# version 18
# baseline (speedup 1.0000x reference)
import math
import os
import numpy as np
import ml_dtypes
import concourse.bass as bass
import concourse.mybir as mybir
from concourse.bass_utils import run_bass_kernel_spmd
from contextlib import ExitStack
from types import SimpleNamespace

F32 = mybir.dt.float32
BF16 = mybir.dt.bfloat16
AF = mybir.ActivationFunctionType
ALU = mybir.AluOpType
AX = mybir.AxisListType

STAGE = 3
KDBG = os.environ.get("KDBG", "")
KJOBS = os.environ.get("KJOBS", "")
NTT = 16
NCH = 8
NJOBS = 34
EPS = 1e-5


def _host_consts():
    lin = np.linspace(np.float32(math.log(1.0 / 32.0)),
                      np.float32(math.log(1.0 / 512.0)), 8, dtype=np.float32)
    gammas = (np.float32(1.0) - np.exp(lin.astype(np.float32))).astype(np.float64)
    l = np.arange(2048, dtype=np.float64)
    i = np.arange(64, dtype=np.float64)
    base = (2 * i + 0.4 * 128) / (1.4 * 128)
    scale = base[None, :] ** (l[:, None] / 512.0)
    invfreq = 1.0 / (10000.0 ** (i / 64.0))
    sinus = l[:, None] * invfreq[None, :]
    sin, cos = np.sin(sinus), np.cos(sinus)
    tabs = np.zeros((8, 128, 4, 16, 64), np.float32)
    for h in range(8):
        gp = gammas[h] ** l
        gn = gammas[h] ** (-l)
        for k, tab in enumerate([cos * scale * gp[:, None], sin * scale * gp[:, None],
                                 (cos / scale) * gn[:, None], (sin / scale) * gn[:, None]]):
            tabs[h, :, k] = tab.astype(np.float32).reshape(16, 128, 64).transpose(1, 0, 2)
    tri = np.triu(np.ones((128, 128), np.float32))
    return tabs, tri


def _prep(inputs):
    f32 = np.float32
    g = lambda k: np.asarray(inputs[k], f32)
    X = np.ascontiguousarray(g("L0_attn_X"))
    sW1, sW2 = g("L0_ffn_sW1")[0], g("L0_ffn_sW2")[0]
    sb1, sb2 = g("L0_ffn_sb1")[0], g("L0_ffn_sb2")[0]
    rW1, rW2 = g("L0_ffn_rW1"), g("L0_ffn_rW2")
    rb1, rb2 = g("L0_ffn_rb1"), g("L0_ffn_rb2")
    gW, gb = g("L0_ffn_gW"), g("L0_ffn_gb")
    WQ, WK, WV = g("L0_attn_W_Q"), g("L0_attn_W_K"), g("L0_attn_W_V")
    WG, WO = g("L0_attn_W_G"), g("L0_attn_W_O")
    gnw, gnb = g("L0_attn_gn_weight"), g("L0_attn_gn_bias")

    w1l, w2l, b1l = [], [], []
    for s in range(4):
        w1l.append(sW1[:, s * 1024:(s + 1) * 1024])
        w2l.append(sW2[s * 1024:(s + 1) * 1024, :])
        b1l.append(sb1[s * 1024:(s + 1) * 1024])
    for e in range(15):
        for hh in range(2):
            w1l.append(rW1[e][:, hh * 1024:(hh + 1) * 1024])
            w2l.append(rW2[e][hh * 1024:(hh + 1) * 1024, :])
            b1l.append(rb1[e][hh * 1024:(hh + 1) * 1024])
    tr = lambda w: w.reshape(8, 128, w.shape[1]).transpose(1, 0, 2)
    com = {}
    com["w1jobs"] = np.ascontiguousarray(np.stack([tr(w) for w in w1l]))
    com["w2jobs"] = np.ascontiguousarray(np.stack([tr(w) for w in w2l]))
    com["b1all"] = np.ascontiguousarray(np.stack([b.reshape(8, 128).T for b in b1l], axis=1))
    com["b2rs"] = np.ascontiguousarray(np.concatenate([rb2, sb2.reshape(1, 1024)], 0))
    com["gw"] = np.ascontiguousarray(tr(gW))
    com["gbv"] = np.ascontiguousarray(gb.reshape(1, 15))
    com["wq"] = np.ascontiguousarray(np.stack([tr(WQ[h]) for h in range(8)]))
    com["wk"] = np.ascontiguousarray(np.stack([tr(WK[h]) for h in range(8)]))
    com["wv"] = np.ascontiguousarray(np.stack([tr(WV[h]) for h in range(8)]))
    com["wgm"] = np.ascontiguousarray(
        np.stack([tr(WG[:, h * 128:(h + 1) * 128]) for h in range(8)]))
    com["wo"] = np.ascontiguousarray(tr(WO))
    com["gnwv"] = np.ascontiguousarray(gnw.reshape(8, 128).T)
    com["gnbv"] = np.ascontiguousarray(gnb.reshape(8, 128).T)
    tabs, _ = _host_consts()
    com["tabs"] = np.ascontiguousarray(tabs)
    return X, com


class Sem:
    __slots__ = ("name", "n", "h")

    def __init__(self, name):
        self.name, self.n, self.h = name, 0, None


class PB:
    ENGS = ("sync", "gpsimd", "vector", "scalar", "tensor")

    def __init__(self):
        self.streams = {e: [] for e in self.ENGS}
        self.sems = []

    def sem(self, name):
        s = Sem(name)
        self.sems.append(s)
        return s

    def wait(self, eng, sem, val):
        if val is None or val <= 0:
            return
        self.streams[eng].append(lambda E, s=sem, v=val: E.wait_ge(s.h, v))

    def do(self, eng, build, inc=None):
        tok = None
        if inc is not None:
            s, step = inc
            s.n += step
            tok = s.n

        def fn(E, b=build, i=inc):
            ins = b(E)
            if i is not None:
                ins.then_inc(i[0].h, i[1])
        self.streams[eng].append(fn)
        return tok

    def barrier(self, skip=()):
        toks = [(s, s.n) for s in self.sems if s.n > 0]
        for e in self.ENGS:
            if e in skip:
                continue
            for s, v in toks:
                self.wait(e, s, v)


def build_program(stage=STAGE):
    nc = bass.Bass("TRN2", target_bir_lowering=False)
    P = PB()

    d = {}
    d["x_in"] = nc.dram_tensor("x_in", [2048, 1024], F32, kind="ExternalInput").ap()
    d["w1jobs"] = nc.dram_tensor("w1jobs", [34, 128, 8, 1024], F32, kind="ExternalInput").ap()
    d["w2jobs"] = nc.dram_tensor("w2jobs", [34, 128, 8, 1024], F32, kind="ExternalInput").ap()
    d["b1all"] = nc.dram_tensor("b1all", [128, 34, 8], F32, kind="ExternalInput").ap()
    d["b2rs"] = nc.dram_tensor("b2rs", [16, 1024], F32, kind="ExternalInput").ap()
    d["gw"] = nc.dram_tensor("gw", [128, 8, 15], F32, kind="ExternalInput").ap()
    d["gbv"] = nc.dram_tensor("gbv", [1, 15], F32, kind="ExternalInput").ap()
    for k in ("wq", "wk", "wv", "wgm"):
        d[k] = nc.dram_tensor(k, [8, 128, 8, 128], F32, kind="ExternalInput").ap()
    d["wo"] = nc.dram_tensor("wo", [128, 8, 1024], F32, kind="ExternalInput").ap()
    d["gnwv"] = nc.dram_tensor("gnwv", [128, 8], F32, kind="ExternalInput").ap()
    d["gnbv"] = nc.dram_tensor("gnbv", [128, 8], F32, kind="ExternalInput").ap()
    d["tabs"] = nc.dram_tensor("tabs", [8, 128, 4, 16, 64], F32, kind="ExternalInput").ap()
    d["out"] = nc.dram_tensor("out", [2048, 1024], F32, kind="ExternalOutput").ap()
    d["xspill"] = nc.dram_tensor("xspill", [128, 16384], F32, kind="Internal").ap()
    d["a2spill"] = nc.dram_tensor("a2spill", [128, 8, 2048], BF16, kind="Internal").ap()
    if KDBG:
        d["dbg_yT"] = nc.dram_tensor("dbg_yT", [128, 8, 2048], BF16, kind="ExternalOutput").ap()
        d["dbg_g05"] = nc.dram_tensor("dbg_g05", [128, 16, 15], F32, kind="ExternalOutput").ap()
        d["dbg_gT"] = nc.dram_tensor("dbg_gT", [16, 2048], F32, kind="ExternalOutput").ap()
        d["dbg_h0"] = nc.dram_tensor("dbg_h0", [128, 8, 512], BF16, kind="ExternalOutput").ap()
        d["dbg_h1"] = nc.dram_tensor("dbg_h1", [128, 8, 512], BF16, kind="ExternalOutput").ap()
        d["dbg_a2t"] = nc.dram_tensor("dbg_a2t", [128, 8, 2048], BF16, kind="ExternalOutput").ap()
        d["dbg_qt"] = nc.dram_tensor("dbg_qt", [128, 2048], BF16, kind="ExternalOutput").ap()
        d["dbg_kt"] = nc.dram_tensor("dbg_kt", [128, 2048], BF16, kind="ExternalOutput").ap()
        d["dbg_vt"] = nc.dram_tensor("dbg_vt", [128, 16, 128], BF16, kind="ExternalOutput").ap()
        d["dbg_ynT"] = nc.dram_tensor("dbg_ynT", [128, 2048], F32, kind="ExternalOutput").ap()
        d["dbg_sTh"] = nc.dram_tensor("dbg_sTh", [128, 2048], F32, kind="ExternalOutput").ap()
        d["dbg_scb"] = nc.dram_tensor("dbg_scb", [128, 16, 512], BF16, kind="ExternalOutput").ap()
        d["dbg_wq"] = nc.dram_tensor("dbg_wq", [128, 8, 128], BF16, kind="ExternalOutput").ap()
        d["dbg_qt2"] = nc.dram_tensor("dbg_qt2", [128, 2048], BF16, kind="ExternalOutput").ap()

    _, tri_np = _host_consts()
    d["identb"] = nc.inline_tensor(np.eye(128, dtype=ml_dtypes.bfloat16), name="identb").ap()
    d["identf"] = nc.inline_tensor(np.eye(128, dtype=np.float32), name="identf").ap()
    d["tri"] = nc.inline_tensor(tri_np, name="tri").ap()

    SV, SS, ST = P.sem("SV"), P.sem("SS"), P.sem("ST")
    DSY, DGP = P.sem("DSY"), P.sem("DGP")
    S_W = [P.sem("S_W0"), P.sem("S_W1")]
    S_X = [P.sem(f"S_X{i}") for i in range(4)]
    S_PRE = P.sem("S_PRE")
    S_AW = [P.sem("S_AW0"), P.sem("S_AW1")]
    S_SPL = P.sem("S_SPL")
    S_WO = P.sem("S_WO")
    S_WOB = P.sem("S_WOB")
    S_ID = P.sem("S_ID")

    E = SimpleNamespace(atile=0, btile=0, gtg=0, gjob=0,
                        tok_gelu={}, tok_btile={}, tok_evac={},
                        tok_gtg_gelu={}, tok_gtg_bdone={}, tok_job_bdone={},
                        last_evac_by_tg={})

    alloc_log = []
    with ExitStack() as OUTER:
        def sball(name, shape, dt, ctx):
            b = nc.sbuf_bytes_remaining
            t = ctx.enter_context(nc.sbuf_tensor(name, shape, dt))
            alloc_log.append((name, b, nc.sbuf_bytes_remaining))
            return t
        for s in P.sems:
            s.h = OUTER.enter_context(nc.semaphore(s.name))

        yT = sball("yT", [128, 8, 2048], BF16, OUTER)
        stats_m = sball("stats_m", [128, 16], F32, OUTER)
        stats_r = sball("stats_r", [128, 16], F32, OUTER)
        stats_nmr = sball("stats_nmr", [128, 16], F32, OUTER)
        scr_ms = sball("scr_ms", [128, 16], F32, OUTER)
        scr_vr = sball("scr_vr", [128, 16], F32, OUTER)
        post_s2 = sball("post_s2", [128, 16], F32, OUTER)
        post_b2 = sball("post_b2", [128, 16], F32, OUTER)
        b1sb = sball("b1sb", [128, 34, 8], F32, OUTER)
        b2rs_sb = sball("b2rs_sb", [16, 1024], F32, OUTER)
        gw_sb = sball("gw_sb", [128, 8, 15], F32, OUTER)
        gb_sb = sball("gb_sb", [1, 15], F32, OUTER)
        gnw_sb = sball("gnw_sb", [128, 8], F32, OUTER)
        gnb_sb = sball("gnb_sb", [128, 8], F32, OUTER)
        ones_sb = sball("ones_sb", [1, 128], F32, OUTER)
        identb_sb = sball("identb_sb", [128, 128], BF16, OUTER)
        identf_sb = sball("identf_sb", [128, 128], F32, OUTER)
        tri_sb = sball("tri_sb", [128, 128], F32, OUTER)

        P.do("vector", lambda V: V.memset(ones_sb[:], 1.0), (SV, 1))

        def emit_ln(x_res, psT0, psT1, y_scr, sq_scr, bf_direct, gate, tok_x=None, pre_y=None):
            tok_red, tok_m, tok_sq, tok_rec, tok_r, tok_nmr = {}, {}, {}, {}, {}, {}
            tok_ytile = {}
            tok_bank = {0: (SV, 0), 1: (SV, 0)}
            state = {"scr_free": 0, "yTlast": 0, "first": True, "yscr_w": 0}
            tok_gmm, tok_exp, tok_vgate = {}, {}, {}
            tok_yT_tg = {}
            ident = identb_sb if bf_direct else identf_sb
            if gate is not None:
                P.do("vector", lambda V: V.memset(gate["gated"][:, 15:16], 1.0), (SV, 1))

            def s_red(tt):
                if tok_x is not None:
                    P.wait("vector", tok_x[tt][0], tok_x[tt][1])
                tok_red[tt] = P.do("vector", lambda V, tt=tt: V.reduce_sum(
                    out=scr_ms[:, tt:tt + 1], in_=x_res[:, tt, :], axis=AX.X), (SV, 1))
                P.wait("vector", SV, tok_red[tt])
                tok_m[tt] = P.do("vector", lambda V, tt=tt: V.tensor_scalar_mul(
                    stats_m[:, tt:tt + 1], scr_ms[:, tt:tt + 1], 1.0 / 1024.0), (SV, 1))

            def s_sq(tt):
                P.wait("scalar", SV, tok_m[tt])
                P.wait("scalar", SS, state["yscr_w"])
                tok_sq[tt] = P.do("scalar", lambda S, tt=tt: S.activation(
                    out=sq_scr[:], in_=x_res[:, tt, :], func=AF.Square,
                    bias=stats_m[:, tt:tt + 1], scale=-1.0,
                    accum_out=scr_vr[:, tt:tt + 1]), (SS, 1))
                state["yscr_w"] = tok_sq[tt]

            def s_var(tt):
                P.wait("vector", SS, tok_sq[tt])
                t1 = P.do("vector", lambda V, tt=tt: V.tensor_scalar(
                    out=scr_vr[:, tt:tt + 1], in0=scr_vr[:, tt:tt + 1],
                    scalar1=1.0 / 1024.0, scalar2=EPS, op0=ALU.mult, op1=ALU.add), (SV, 1))
                P.wait("vector", SV, t1)
                tok_rec[tt] = P.do("vector", lambda V, tt=tt: V.reciprocal(
                    scr_vr[:, tt:tt + 1], scr_vr[:, tt:tt + 1]), (SV, 1))

            def s_sqrt(tt):
                P.wait("scalar", SV, tok_rec[tt])
                tok_r[tt] = P.do("scalar", lambda S, tt=tt: S.activation(
                    out=stats_r[:, tt:tt + 1], in_=scr_vr[:, tt:tt + 1],
                    func=AF.Sqrt), (SS, 1))

            def s_nmr(tt):
                P.wait("vector", SS, tok_r[tt])
                tok_nmr[tt] = P.do("vector", lambda V, tt=tt: V.tensor_scalar(
                    out=stats_nmr[:, tt:tt + 1], in0=stats_m[:, tt:tt + 1],
                    scalar1=stats_r[:, tt:tt + 1], scalar2=-1.0,
                    op0=ALU.mult, op1=ALU.mult), (SV, 1))

            def s_y(tt):
                if state["first"]:
                    state["first"] = False
                    if pre_y:
                        for eng_, sem_, val_ in pre_y:
                            P.wait(eng_, sem_, val_)
                P.wait("scalar", SV, tok_nmr[tt])
                P.wait("scalar", SS, state["yscr_w"])
                P.wait("scalar", ST, state["scr_free"])
                tok_ytile[tt] = P.do("scalar", lambda S, tt=tt: S.activation(
                    out=y_scr[:], in_=x_res[:, tt, :], func=AF.Identity,
                    bias=stats_nmr[:, tt:tt + 1], scale=stats_r[:, tt:tt + 1]), (SS, 1))
                state["yscr_w"] = tok_ytile[tt]
                for c in range(NCH):
                    bank = c % 2
                    ps = psT0 if bank == 0 else psT1
                    if tt == 0 and c == 0:
                        P.wait("tensor", S_ID, S_ID.n)
                    P.wait("tensor", SS, tok_ytile[tt])
                    P.wait("tensor", tok_bank[bank][0], tok_bank[bank][1])
                    ttr = P.do("tensor", lambda T, ps=ps, c=c: T.transpose(
                        ps[:], y_scr[:, c * 128:(c + 1) * 128], ident[:]), (ST, 1))
                    state["scr_free"] = ttr
                    P.wait("vector", ST, ttr)
                    tcp = P.do("vector", lambda V, ps=ps, tt=tt, c=c: V.tensor_copy(
                        out=yT[:, c, tt * 128:(tt + 1) * 128], in_=ps[:]), (SV, 1))
                    state["yTlast"] = tcp
                    tok_bank[bank] = (SV, tcp)
                    if gate is not None:
                        P.wait("scalar", ST, tok_gmm.get(tt - 1, 0))
                        P.wait("scalar", SV, tcp)
                        tcs = P.do("scalar", lambda S, ps=ps, c=c: S.activation(
                            out=gate["yTf"][:, c, :], in_=ps[:], func=AF.Identity),
                            (SS, 1))
                        tok_bank[bank] = (SS, tcs)
                if tt % 4 == 3:
                    tok_yT_tg[tt // 4] = state["yTlast"]
                if gate is None:
                    return
                pG, pGT = gate["pG"], gate["pGT"]
                if tt == 0:
                    P.wait("tensor", S_PRE, S_PRE.n)
                P.wait("tensor", tok_bank[0][0], tok_bank[0][1])
                P.wait("tensor", tok_bank[1][0], tok_bank[1][1])
                P.wait("tensor", SS, tok_exp.get(tt - 1, 0))
                for c in range(NCH):
                    P.do("tensor", lambda T, c=c: T.matmul(
                        pG[:], gate["yTf"][:, c, :], gw_sb[:, c, :],
                        start=(c == 0), stop=False))
                tok_gmm[tt] = P.do("tensor", lambda T: T.matmul(
                    pG[:], ones_sb[:], gb_sb[:], start=False, stop=True), (ST, 1))
                P.wait("vector", ST, tok_gmm[tt])
                tnm = P.do("vector", lambda V: V.reduce_max(
                    out=gate["nmax"][:], in_=pG[:], axis=AX.X, negate=True), (SV, 1))
                P.wait("scalar", SV, tnm)
                P.wait("scalar", SV, tok_vgate.get(tt - 1, 0))
                tok_exp[tt] = P.do("scalar", lambda S: S.activation(
                    out=gate["e15"][:], in_=pG[:], func=AF.Exp,
                    bias=gate["nmax"][:], scale=1.0), (SS, 1))
                P.wait("vector", SS, tok_exp[tt])
                P.wait("vector", SV, tok_vgate.get(tt - 1, 0))
                t = P.do("vector", lambda V: V.reduce_sum(
                    out=gate["ssum"][:], in_=gate["e15"][:], axis=AX.X), (SV, 1))
                P.wait("vector", SV, t)
                t = P.do("vector", lambda V: V.reciprocal(
                    gate["ssum"][:], gate["ssum"][:]), (SV, 1))
                P.wait("vector", SV, t)
                t = P.do("vector", lambda V: V.tensor_scalar(
                    out=gate["w15"][:], in0=gate["e15"][:], scalar1=gate["ssum"][:],
                    scalar2=None, op0=ALU.mult), (SV, 1))
                P.wait("vector", SV, t)
                t = P.do("vector", lambda V: V.max(
                    out=gate["t8"][:], in_=gate["w15"][:]), (SV, 1))
                P.wait("vector", SV, t)
                t = P.do("vector", lambda V: V.tensor_scalar(
                    out=gate["mask"][:], in0=gate["w15"][:],
                    scalar1=gate["t8"][:, 2:3], scalar2=None, op0=ALU.is_ge), (SV, 1))
                P.wait("vector", SV, t)
                t = P.do("vector", lambda V, tt=tt: V.scalar_tensor_tensor(
                    out=gate["g05"][:, tt, :], in0=gate["mask"][:], scalar=0.5,
                    in1=gate["w15"][:], op0=ALU.mult, op1=ALU.mult), (SV, 1))
                P.wait("vector", SV, t)
                t = P.do("vector", lambda V: V.tensor_mul(
                    gate["gated"][:, 0:15], gate["mask"][:], gate["w15"][:]), (SV, 1))
                P.wait("tensor", SV, t)
                P.wait("tensor", SV, tok_vgate.get(tt - 1, 0))
                tg2 = P.do("tensor", lambda T: T.transpose(
                    pGT[:], gate["gated"][:], identf_sb[:]), (ST, 1))
                P.wait("vector", ST, tg2)
                tok_vgate[tt] = P.do("vector", lambda V, tt=tt: V.tensor_copy(
                    out=gate["gatedT"][0:16, tt * 128:(tt + 1) * 128],
                    in_=pGT[:]), (SV, 1))

            for step in range(NTT + 3):
                if step < NTT:
                    s_red(step)
                if 0 <= step - 1 < NTT:
                    s_sq(step - 1)
                    s_var(step - 1)
                if 0 <= step - 2 < NTT:
                    s_sqrt(step - 2)
                    s_nmr(step - 2)
                if 0 <= step - 3 < NTT:
                    s_y(step - 3)
            return SimpleNamespace(
                tok_y_last=tok_ytile[NTT - 1],
                tok_yT=state["yTlast"],
                tok_yT_tg=tok_yT_tg,
                tok_gate=tok_vgate.get(NTT - 1, 0),
                tok_tr_last=state["scr_free"])

        def emit_ffn_call(call, x_res, RF, dma_gate=None, tok_x=None):
            w1b = [RF["w1b0"], RF["w1b1"]]
            w2b = [RF["w2b0"], RF["w2b1"]]
            hb = [RF["h0"], RF["h1"]]
            pA = [RF["pA0"], RF["pA1"]]
            pB = [RF["pB0"], RF["pB1"]]
            gate = {k: RF[k] for k in ("yTf", "pG", "pGT", "nmax", "e15", "ssum",
                                       "w15", "t8", "mask", "gated", "g05", "gatedT")}
            ln = emit_ln(x_res, RF["pT0"], RF["pT1"], RF["y_scr"], RF["y_scr"], False, gate,
                         tok_x=tok_x)

            tok_pb = {}
            for tt in range(NTT):
                P.do("vector", lambda V, tt=tt: V.tensor_scalar(
                    out=post_s2[:, tt:tt + 1], in0=stats_r[:, tt:tt + 1],
                    scalar1=0.5, scalar2=1.0, op0=ALU.mult, op1=ALU.add), (SV, 1))
                tok_pb[tt] = P.do("vector", lambda V, tt=tt: V.tensor_scalar_mul(
                    post_b2[:, tt:tt + 1], stats_nmr[:, tt:tt + 1], 0.5), (SV, 1))
            tok_post = {}
            for tt in range(NTT):
                P.wait("scalar", SV, tok_pb[tt])
                tok_post[tt] = P.do("scalar", lambda S, tt=tt: S.activation(
                    out=x_res[:, tt, :], in_=x_res[:, tt, :], func=AF.Identity,
                    bias=post_b2[:, tt:tt + 1], scale=post_s2[:, tt:tt + 1]), (SS, 1))
            tok_post_last = tok_post[NTT - 1]
            E.last_tok_post = tok_post

            if KJOBS == "shared":
                job_list = [0, 1, 2, 3]
            elif KJOBS.startswith("j"):
                job_list = [int(t) for t in KJOBS[1:].split(",")]
            else:
                job_list = list(range(NJOBS))
            tok_w = {}

            def emit_wdma(jj, j):
                gj = E.gjob
                if dma_gate is not None and jj <= 1:
                    for s_, v_ in dma_gate.get("w1", []):
                        P.wait("gpsimd", s_, v_)
                if gj >= 2:
                    P.wait("gpsimd", ST, E.tok_job_bdone[gj - 2])
                P.do("gpsimd", lambda G, j=j: G.dma_start(
                    out=w1b[j % 2][:], in_=d["w1jobs"][j]), (S_W[j % 2], 16))
                if dma_gate is not None and jj <= 1:
                    for s_, v_ in dma_gate.get("w2", []):
                        P.wait("gpsimd", s_, v_)
                tok_w[j] = P.do("gpsimd", lambda G, j=j: G.dma_start(
                    out=w2b[j % 2][:], in_=d["w2jobs"][j]), (S_W[j % 2], 16))

            first_evac = [True]

            def emit_A(j, tg, gtg):
                war = E.tok_gtg_bdone.get(gtg - 2, 0)
                for ft in range(NCH):
                    at = E.atile
                    P.wait("tensor", SS, E.tok_gelu.get(at - 2, 0))
                    if ft == 0 and tg == 0:
                        P.wait("tensor", S_W[j % 2], tok_w[j])
                    if ft == 0 and j == job_list[0]:
                        P.wait("tensor", SV, ln.tok_yT_tg[tg])
                    tok = None
                    for c in range(NCH):
                        tok = P.do("tensor", lambda T, j=j, tg=tg, ft=ft, c=c, at=at:
                                   T.matmul(pA[at % 2][:],
                                            w1b[j % 2][:, c, ft * 128:(ft + 1) * 128],
                                            yT[:, c, tg * 512:(tg + 1) * 512],
                                            start=(c == 0), stop=(c == NCH - 1)),
                                   (ST, 1) if c == NCH - 1 else None)
                    P.wait("scalar", ST, tok)
                    if at == 0:
                        P.wait("scalar", S_PRE, S_PRE.n)
                    if ft == 0:
                        P.wait("scalar", ST, war)
                    E.tok_gelu[at] = P.do("scalar", lambda S, j=j, ft=ft, at=at, gtg=gtg:
                                          S.activation(
                        out=hb[gtg % 2][:, ft, :], in_=pA[at % 2][:], func=AF.Gelu,
                        bias=b1sb[:, j, ft:ft + 1], scale=1.0), (SS, 1))
                    E.atile += 1
                E.tok_gtg_gelu[gtg] = E.tok_gelu[E.atile - 1]

            def emit_B(j, tg, gtg, kind, eidx):
                P.wait("tensor", SS, E.tok_gtg_gelu[gtg])
                has_bias = (kind == "shared" and j == 3)
                nmm = NCH + (1 if has_bias else 0)
                for tl in range(4):
                    ttg = tg * 4 + tl
                    for dg in range(2):
                        bt = E.btile
                        P.wait("tensor", SV, E.tok_evac.get(bt - 2, 0))
                        tok = None
                        for k in range(nmm):
                            il = (k == nmm - 1)
                            if k < NCH:
                                tok = P.do("tensor",
                                           lambda T, tl=tl, dg=dg, ft=k, bt=bt, il=il,
                                           gtg=gtg, j=j: T.matmul(
                                               pB[bt % 2][:],
                                               hb[gtg % 2][:, ft, tl * 128:(tl + 1) * 128],
                                               w2b[j % 2][:, ft, dg * 512:(dg + 1) * 512],
                                               start=(ft == 0), stop=il),
                                           (ST, 1) if il else None)
                            else:
                                if ttg == 0 and dg == 0:
                                    P.wait("tensor", S_PRE, S_PRE.n)
                                    P.wait("tensor", SV, ln.tok_gate)
                                tok = P.do("tensor",
                                           lambda T, ttg=ttg, dg=dg, bt=bt, il=il:
                                           T.matmul(
                                               pB[bt % 2][:],
                                               RF["gatedT"][0:16, ttg * 128:(ttg + 1) * 128],
                                               b2rs_sb[0:16, dg * 512:(dg + 1) * 512],
                                               start=False, stop=il),
                                           (ST, 1) if il else None)
                        E.tok_btile[bt] = tok
                        P.wait("vector", ST, tok)
                        if first_evac[0]:
                            P.wait("vector", SS, tok_post_last)
                            P.wait("vector", SV, ln.tok_gate)
                            first_evac[0] = False
                        if kind == "shared":
                            ev = P.do("vector", lambda V, ttg=ttg, dg=dg, bt=bt:
                                      V.scalar_tensor_tensor(
                                          out=x_res[:, ttg, dg * 512:(dg + 1) * 512],
                                          in0=pB[bt % 2][:], scalar=0.5,
                                          in1=x_res[:, ttg, dg * 512:(dg + 1) * 512],
                                          op0=ALU.mult, op1=ALU.add), (SV, 1))
                        else:
                            ev = P.do("vector", lambda V, ttg=ttg, dg=dg, bt=bt, e=eidx:
                                      V.scalar_tensor_tensor(
                                          out=x_res[:, ttg, dg * 512:(dg + 1) * 512],
                                          in0=pB[bt % 2][:],
                                          scalar=RF["g05"][:, ttg, e:e + 1],
                                          in1=x_res[:, ttg, dg * 512:(dg + 1) * 512],
                                          op0=ALU.mult, op1=ALU.add), (SV, 1))
                        E.tok_evac[bt] = ev
                        E.last_evac_by_tg[ttg // 4] = ev
                        E.btile += 1

            for jj, j in enumerate(job_list):
                emit_wdma(jj, j)
                kind = "shared" if j < 4 else "router"
                eidx = None if j < 4 else (j - 4) // 2
                gtg_of = {}
                for what, tg in [("A", 0), ("A", 1), ("B", 0), ("A", 2), ("B", 1),
                                 ("A", 3), ("B", 2), ("B", 3)]:
                    if what == "A":
                        gtg_of[tg] = E.gtg
                        emit_A(j, tg, E.gtg)
                        E.gtg += 1
                    else:
                        emit_B(j, tg, gtg_of[tg], kind, eidx)
                        E.tok_gtg_bdone[gtg_of[tg]] = E.tok_btile[E.btile - 1]
                E.tok_job_bdone[E.gjob] = E.tok_btile[E.btile - 1]
                E.gjob += 1

            tok_all_evac = E.tok_evac[E.btile - 1]
            return tok_all_evac

        def alloc_RF(ctx, pfx):
            RF = {}
            RF["w2b0"] = sball(pfx + "w2b0", [128, 8, 1024], BF16, ctx)
            RF["w2b1"] = sball(pfx + "w2b1", [128, 8, 1024], BF16, ctx)
            RF["h0"] = sball(pfx + "h0", [128, 8, 512], BF16, ctx)
            RF["h1"] = sball(pfx + "h1", [128, 8, 512], BF16, ctx)
            RF["y_scr"] = sball(pfx + "y_scr", [128, 1024], F32, ctx)
            RF["yTf"] = sball(pfx + "yTf", [128, 8, 128], F32, ctx)
            RF["nmax"] = sball(pfx + "nmax", [128, 1], F32, ctx)
            RF["e15"] = sball(pfx + "e15", [128, 15], F32, ctx)
            RF["ssum"] = sball(pfx + "ssum", [128, 1], F32, ctx)
            RF["w15"] = sball(pfx + "w15", [128, 15], F32, ctx)
            RF["t8"] = sball(pfx + "t8", [128, 8], F32, ctx)
            RF["mask"] = sball(pfx + "mask", [128, 15], F32, ctx)
            RF["gated"] = sball(pfx + "gated", [128, 16], F32, ctx)
            RF["g05"] = sball(pfx + "g05", [128, 16, 15], F32, ctx)
            RF["gatedT"] = sball(pfx + "gatedT", [16, 2048], F32, ctx)
            RF["w1b0"] = sball(pfx + "w1b0", [128, 8, 1024], BF16, ctx)
            RF["w1b1"] = sball(pfx + "w1b1", [128, 8, 1024], BF16, ctx)
            for nm, shape, dt in [("pA0", [128, 512], F32), ("pA1", [128, 512], F32),
                                  ("pB0", [128, 512], F32), ("pB1", [128, 512], F32),
                                  ("pT0", [128, 128], F32), ("pT1", [128, 128], F32),
                                  ("pG", [128, 15], F32), ("pGT", [16, 128], F32)]:
                RF[nm] = ctx.enter_context(nc.psum_tensor(pfx + nm, shape, dt))
            return RF

        with ExitStack() as R1:
            x_res = sball("x_res", [128, 16, 1024], F32, R1)
            xre = d["x_in"].rearrange("(tt p) d -> p tt d", p=128)
            tok_x = {}
            for a_, b_ in [(d["identb"], identb_sb), (d["identf"], identf_sb),
                           (d["tri"], tri_sb)]:
                P.do("sync", lambda S, a=a_, b=b_: S.dma_start(
                    out=b[:], in_=a), (S_ID, 16))
            pre1 = [(d["gw"], gw_sb), (d["gbv"], gb_sb), (d["b1all"], b1sb),
                    (d["b2rs"], b2rs_sb)]
            pre2 = [(d["gnwv"], gnw_sb), (d["gnbv"], gnb_sb)]
            for q in range(4):
                t = P.do("sync", lambda S, q=q: S.dma_start(
                    out=x_res[:, q * 4:(q + 1) * 4, :], in_=xre[:, q * 4:(q + 1) * 4, :]),
                    (S_X[q], 16))
                for tt in range(q * 4, (q + 1) * 4):
                    tok_x[tt] = (S_X[q], t)
                batch = pre1 if q == 0 else (pre2 if q == 1 else [])
                for a_, b_ in batch:
                    P.do("sync", lambda S, a=a_, b=b_: S.dma_start(
                        out=b[:], in_=a), (S_PRE, 16))

            with ExitStack() as RFc:
                RF = alloc_RF(RFc, "f1")
                emit_ffn_call(0, x_res, RF, tok_x=tok_x,
                              dma_gate={"w1": [(S_X[0], 16)]})
            P.barrier(skip=("vector", "scalar", "tensor", "sync"))

            E.f1_done = dict(jb=E.tok_job_bdone[E.gjob - 1],
                             ev=E.tok_evac[E.btile - 1],
                             ge=E.tok_gelu[E.atile - 1],
                             po=E.last_tok_post[NTT - 1],
                             evtg=dict(E.last_evac_by_tg))
            if stage == 1:
                P.barrier(skip=("sync",))
                if KDBG:
                    P.do("sync", lambda S: S.dma_start(out=d["dbg_h0"], in_=RF["h0"][:]), (DSY, 16))
                    P.do("sync", lambda S: S.dma_start(out=d["dbg_h1"], in_=RF["h1"][:]), (DSY, 16))
                    P.do("sync", lambda S: S.dma_start(out=d["dbg_yT"], in_=yT[:]), (DSY, 16))
                    P.do("sync", lambda S: S.dma_start(out=d["dbg_g05"], in_=RF["g05"][:]), (DSY, 16))
                    P.do("sync", lambda S: S.dma_start(out=d["dbg_gT"], in_=RF["gatedT"][:]), (DSY, 16))
                for q in range(4):
                    P.wait("sync", SV, E.f1_done["evtg"][q])
                    P.wait("sync", SS, E.f1_done["po"])
                    tok_out = P.do("sync", lambda S, q=q: S.dma_start(
                        out=d["out"].rearrange("(tt p) d -> p tt d", p=128)[
                            :, q * 4:(q + 1) * 4, :],
                        in_=x_res[:, q * 4:(q + 1) * 4, :]), (DSY, 16))
                P.wait("sync", DSY, tok_out)
            else:
                fd = E.f1_done
                tok_x2 = {tt: (SV, fd["evtg"][tt // 4]) for tt in range(NTT)}
                with ExitStack() as RL2:
                    y2_scr = sball("y2_scr", [128, 1024], BF16, RL2)
                    sq2_scr = sball("sq2_scr", [128, 1024], F32, RL2)
                    pL0 = RL2.enter_context(nc.psum_tensor("pL0", [128, 128], BF16))
                    pL1 = RL2.enter_context(nc.psum_tensor("pL1", [128, 128], BF16))
                    P.wait("scalar", ST, fd["jb"])
                    P.wait("tensor", SV, fd["ev"])
                    P.wait("tensor", SS, fd["ge"])
                    ln2 = emit_ln(x_res, pL0, pL1, y2_scr, sq2_scr, True, None,
                                  tok_x=tok_x2,
                                  pre_y=[("vector", ST, fd["jb"])])
                    E.ln2_yT = ln2.tok_yT
                for q in range(4):
                    P.wait("sync", SV, fd["evtg"][q])
                    P.wait("sync", SS, fd["po"])
                    P.do("sync", lambda S, q=q: S.dma_start(
                        out=d["xspill"][:, q * 4096:(q + 1) * 4096],
                        in_=x_res[:, q * 4:(q + 1) * 4, :]), (S_SPL, 16))

        if stage >= 2:
            with ExitStack() as RA:
                A2T = sball("A2T", [128, 8, 2048], BF16, RA)
                QT = sball("QT", [128, 2048], BF16, RA)
                KT = sball("KT", [128, 2048], BF16, RA)
                Vt = sball("Vt", [128, 16, 128], BF16, RA)
                scb = [sball("scb0", [128, 16, 512], BF16, RA),
                       sball("scb1", [128, 16, 512], BF16, RA)]
                ynT = sball("ynT", [128, 2048], F32, RA)
                sTh = sball("sTh", [128, 2048], F32, RA)
                q32 = [sball("q32a", [128, 512], F32, RA),
                       sball("q32b", [128, 512], F32, RA)]
                qbf = [sball("qbfa", [128, 512], BF16, RA),
                       sball("qbfb", [128, 512], BF16, RA)]
                xa4 = sball("xa4", [128, 256], F32, RA)
                xb4 = sball("xb4", [128, 256], F32, RA)
                ynscr = [sball("ynscr0", [128, 128], F32, RA),
                         sball("ynscr1", [128, 128], F32, RA)]
                gnm = sball("gnm", [128, 1], F32, RA)
                gnv = sball("gnv", [128, 1], F32, RA)
                gnr = sball("gnr", [128, 1], F32, RA)
                gnnmr = sball("gnnmr", [128, 1], F32, RA)
                gnm2 = sball("gnm2", [128, 1], F32, RA)
                tabb = [sball("tabb0", [128, 4, 16, 64], F32, RA),
                        sball("tabb1", [128, 4, 16, 64], F32, RA)]
                wqb = [sball("wqb0", [128, 8, 128], BF16, RA),
                       sball("wqb1", [128, 8, 128], BF16, RA)]
                wkb = [sball("wkb0", [128, 8, 128], BF16, RA),
                       sball("wkb1", [128, 8, 128], BF16, RA)]
                wvb = [sball("wvb0", [128, 8, 128], BF16, RA),
                       sball("wvb1", [128, 8, 128], BF16, RA)]
                wgb = [sball("wgb0", [128, 8, 128], BF16, RA),
                       sball("wgb1", [128, 8, 128], BF16, RA)]
                pf = [RA.enter_context(nc.psum_tensor(f"pf{i}", [128, 512], F32))
                      for i in range(6)]
                pb2 = [RA.enter_context(nc.psum_tensor(f"pb{i}", [128, 128], BF16))
                       for i in range(2)]
                pftok = [[] for _ in range(6)]
                pbtok = [[] for _ in range(2)]

                def bwait(eng, toks):
                    for s, v in toks:
                        P.wait(eng, s, v)

                def bwait_merged(eng, banklists):
                    best = {}
                    for lst in banklists:
                        for s, v in lst:
                            if v > best.get(id(s), (None, -1))[1]:
                                best[id(s)] = (s, v)
                    for s, v in best.values():
                        P.wait(eng, s, v)

                qev, qov, bev, bov = [], [], [], []
                for par in range(2):
                    v32 = q32[par][:].rearrange("p (t e) -> p t e", t=4)
                    vbf = qbf[par][:].rearrange("p (t e) -> p t e", t=4)
                    qev.append(v32[:, :, 0:128:2])
                    qov.append(v32[:, :, 1:128:2])
                    bev.append(vbf[:, :, 0:128:2])
                    bov.append(vbf[:, :, 1:128:2])
                xa4v = xa4[:].rearrange("p (t e) -> p t e", t=4)
                xb4v = xb4[:].rearrange("p (t e) -> p t e", t=4)

                tok_head_pe, tok_head_v, tok_head_s, tok_a2t = {}, {}, {}, {}
                qkv_carry = {"ttr_last": [0, 0], "chain_last": [0, 0]}
                gn_prev_global = [0]
                ydone_carry = [0, 0]
                yntr2_carry = [0, 0]

                for h in range(8):
                    if h == 0:
                        P.wait("vector", S_SPL, S_SPL.n)
                        P.wait("scalar", S_SPL, S_SPL.n)
                        P.wait("tensor", SV, max(E.f1_done["ev"], E.ln2_yT))
                        P.wait("tensor", SS, E.f1_done["ge"])
                    if h >= 2:
                        P.wait("gpsimd", ST, tok_head_pe.get(h - 2, 0))
                        P.wait("gpsimd", SV, tok_head_v.get(h - 2, 0))
                        P.wait("gpsimd", SS, tok_head_s.get(h - 2, 0))
                    P.do("gpsimd", lambda G, h=h: G.dma_start(
                        out=tabb[h % 2][:], in_=d["tabs"][h]), (S_AW[h % 2], 16))
                    for src_, dstl in [("wq", wqb), ("wk", wkb), ("wv", wvb),
                                       ("wgm", wgb)]:
                        P.do("gpsimd", lambda G, s=src_, dl=dstl, h=h: G.dma_start(
                            out=dl[h % 2][:], in_=d[s][h]), (S_AW[h % 2], 16))
                    tok_hw = S_AW[h % 2].n

                    P.wait("tensor", S_AW[h % 2], tok_hw)

                    tok_silu = {}
                    for g4 in range(4):
                        bk = 4 + (g4 % 2)
                        bwait("tensor", pftok[bk])
                        tok = None
                        for c in range(NCH):
                            tok = P.do("tensor", lambda T, g4=g4, c=c, h=h, bk=bk:
                                       T.matmul(pf[bk][:], wgb[h % 2][:, c, :],
                                                yT[:, c, g4 * 512:(g4 + 1) * 512],
                                                start=(c == 0), stop=(c == NCH - 1)),
                                       (ST, 1) if c == NCH - 1 else None)
                        P.wait("scalar", ST, tok)
                        if g4 == 0:
                            P.wait("scalar", SV, tok_a2t.get(h - 1, 0))
                        tok_silu[g4] = P.do("scalar", lambda S, g4=g4, bk=bk:
                                            S.activation(
                            out=sTh[:, g4 * 512:(g4 + 1) * 512],
                            in_=pf[bk][:], func=AF.Silu), (SS, 1))
                        pftok[bk] = [(SS, tok_silu[g4])]

                    def qkv2(which, wl, dstT, kc, ks):
                        chain_tok = {}
                        ttr = {}
                        last_sv = [0]

                        def emit_mms(grp):
                            stage_last = 0
                            bwait_merged("tensor", [pftok[0], pftok[1],
                                                    pftok[2], pftok[3]])
                            for t4 in range(4):
                                tt = grp * 4 + t4
                                bk = tt % 4
                                tok = None
                                for c in range(NCH):
                                    tok = P.do("tensor", lambda T, c=c, tt=tt, bk=bk,
                                               h=h, wl=wl: T.matmul(
                                        pf[bk][:, 0:128],
                                        yT[:, c, tt * 128:(tt + 1) * 128],
                                        wl[h % 2][:, c, :],
                                        start=(c == 0), stop=(c == NCH - 1)),
                                        (ST, 1) if c == NCH - 1 else None)
                                if which == "v":
                                    P.wait("vector", ST, tok)
                                    t = P.do("vector", lambda V, tt=tt, bk=bk:
                                             V.tensor_copy(out=Vt[:, tt, :],
                                                           in_=pf[bk][:, 0:128]),
                                             (SV, 1))
                                    pftok[bk] = [(SV, t)]
                                    last_sv[0] = t
                                else:
                                    P.wait("scalar", ST, tok)
                                    if t4 == 0:
                                        prev = chain_tok.get(grp - 2,
                                                             qkv_carry["chain_last"][grp % 2])
                                        P.wait("scalar", SV, prev)
                                    t = P.do("scalar", lambda S, t4=t4, bk=bk, grp=grp:
                                             S.activation(
                                        out=q32[grp % 2][:, t4 * 128:(t4 + 1) * 128],
                                        in_=pf[bk][:, 0:128], func=AF.Identity),
                                        (SS, 1))
                                    pftok[bk] = [(SS, t)]
                                    stage_last = t
                            return stage_last

                        def emit_chain(grp, stage_last):
                            par = grp % 2
                            ct = tabb[h % 2][:, kc, grp * 4:(grp + 1) * 4, :]
                            st_ = tabb[h % 2][:, ks, grp * 4:(grp + 1) * 4, :]
                            P.wait("vector", SS, stage_last)
                            P.wait("vector", SV, qkv_carry.get("prev_chain", 0))
                            P.wait("vector", SV, chain_tok.get(grp - 2,
                                                               qkv_carry["chain_last"][par]))
                            P.wait("vector", ST, ttr.get((grp - 2) * 4 + 3,
                                                         qkv_carry["ttr_last"][par]))
                            P.do("vector", lambda V, ct=ct, par=par: V.tensor_mul(
                                xa4v, qev[par], ct), (SV, 1))
                            t2 = P.do("vector", lambda V, st_=st_, par=par: V.tensor_mul(
                                xb4v, qov[par], st_), (SV, 1))
                            P.wait("vector", SV, t2)
                            t3 = P.do("vector", lambda V, par=par: V.tensor_sub(
                                bev[par], xa4v, xb4v), (SV, 1))
                            P.wait("vector", SV, t3)
                            P.do("vector", lambda V, ct=ct, par=par: V.tensor_mul(
                                xa4v, qov[par], ct), (SV, 1))
                            t5 = P.do("vector", lambda V, st_=st_, par=par: V.tensor_mul(
                                xb4v, qev[par], st_), (SV, 1))
                            P.wait("vector", SV, t5)
                            chain_tok[grp] = P.do("vector", lambda V, par=par: V.tensor_add(
                                bov[par], xa4v, xb4v), (SV, 1))
                            qkv_carry["prev_chain"] = chain_tok[grp]
                            last_sv[0] = chain_tok[grp]

                        def emit_trans(grp):
                            for t4 in range(4):
                                tt = grp * 4 + t4
                                P.wait("tensor", SV, chain_tok[grp])
                                bwait("tensor", pbtok[tt % 2])
                                tk = P.do("tensor", lambda T, t4=t4, tt=tt, grp=grp:
                                          T.transpose(
                                    pb2[tt % 2][:],
                                    qbf[grp % 2][:, t4 * 128:(t4 + 1) * 128],
                                    identb_sb[:]), (ST, 1))
                                ttr[tt] = tk
                                P.wait("vector", ST, tk)
                                tc = P.do("vector", lambda V, tt=tt, dT=dstT:
                                          V.tensor_copy(
                                    out=dT[:, tt * 128:(tt + 1) * 128],
                                    in_=pb2[tt % 2][:]), (SV, 1))
                                pbtok[tt % 2] = [(SV, tc)]
                                last_sv[0] = tc

                        if which == "v":
                            for grp in range(4):
                                emit_mms(grp)
                        else:
                            sl = {}
                            sl[0] = emit_mms(0)
                            emit_chain(0, sl[0])
                            for grp in range(1, 4):
                                sl[grp] = emit_mms(grp)
                                emit_chain(grp, sl[grp])
                                emit_trans(grp - 1)
                            emit_trans(3)
                            qkv_carry["ttr_last"] = [ttr[11], ttr[15]]
                            qkv_carry["chain_last"] = [chain_tok[2], chain_tok[3]]
                        return last_sv[0]

                    tq = qkv2("q", wqb, QT, 0, 1)
                    tk2 = qkv2("k", wkb, KT, 2, 3)
                    tv = qkv2("v", wvb, None, None, None)

                    sc_cnt = 0
                    tok_ydone = {-2: ydone_carry[0], -1: ydone_carry[1]}
                    tok_yncons = {}
                    tok_yntrv = {}
                    tok_gn_prev = gn_prev_global
                    cast_sv = {}
                    cast_ss = {}
                    first_sc = [True]

                    def emit_scores(g):
                        nonlocal sc_cnt
                        cast_sv[g] = 0
                        cast_ss[g] = 0
                        for i in range(4 * g + 4):
                            bk = 4 + (sc_cnt % 2)
                            bwait("tensor", pftok[bk])
                            if first_sc[0]:
                                P.wait("tensor", SV, max(tq, tk2, tv))
                                first_sc[0] = False
                            if i == 0:
                                P.wait("tensor", ST, tok_ydone.get(g - 2, 0))
                            tokm = P.do("tensor", lambda T, i=i, g=g, bk=bk: T.matmul(
                                pf[bk][:], KT[:, i * 128:(i + 1) * 128],
                                QT[:, g * 512:(g + 1) * 512],
                                start=True, stop=True), (ST, 1))
                            gl = g * 4
                            frees = []
                            if i >= gl:
                                off = (i - gl) * 128
                                P.wait("vector", ST, tokm)
                                tdi = P.do("vector", lambda V, i=i, g=g, bk=bk, off=off:
                                           V.tensor_mul(
                                    scb[g % 2][:, i, off:off + 128],
                                    pf[bk][:, off:off + 128], tri_sb[:]), (SV, 1))
                                cast_sv[g] = tdi
                                frees.append((SV, tdi))
                                if off + 128 < 512:
                                    P.wait("scalar", SV, tdi)
                                    tcp = P.do("scalar", lambda S, i=i, g=g, bk=bk,
                                               off=off: S.activation(
                                        out=scb[g % 2][:, i, off + 128:512],
                                        in_=pf[bk][:, off + 128:512],
                                        func=AF.Identity), (SS, 1))
                                    cast_ss[g] = tcp
                                    frees = [(SS, tcp)]
                            else:
                                if sc_cnt % 2 == 0:
                                    P.wait("vector", ST, tokm)
                                    t = P.do("vector", lambda V, i=i, g=g, bk=bk:
                                             V.tensor_copy(
                                        out=scb[g % 2][:, i, 0:512],
                                        in_=pf[bk][:, 0:512]), (SV, 1))
                                    cast_sv[g] = t
                                    frees.append((SV, t))
                                else:
                                    P.wait("scalar", ST, tokm)
                                    t = P.do("scalar", lambda S, i=i, g=g, bk=bk:
                                             S.activation(
                                        out=scb[g % 2][:, i, 0:512],
                                        in_=pf[bk][:, 0:512], func=AF.Identity),
                                        (SS, 1))
                                    cast_ss[g] = t
                                    frees.append((SS, t))
                            pftok[bk] = frees
                            sc_cnt += 1

                    def emit_Y(g):
                        for jl in range(4):
                            j_ = g * 4 + jl
                            ybk = j_ % 3
                            bwait("tensor", pftok[ybk])
                            if jl == 0:
                                P.wait("tensor", SV, cast_sv[g])
                                P.wait("tensor", SS, cast_ss[g])
                            tok = None
                            for i in range(j_ + 1):
                                tok = P.do("tensor", lambda T, i=i, jl=jl, g=g,
                                           ybk=ybk, j_=j_: T.matmul(
                                    pf[ybk][:, 0:128],
                                    scb[g % 2][:, i, jl * 128:(jl + 1) * 128],
                                    Vt[:, i, :],
                                    start=(i == 0), stop=(i == j_)),
                                    (ST, 1) if i == j_ else None)
                            if jl == 3:
                                tok_ydone[g] = tok
                            pY_ = pf[ybk][:, 0:128]
                            P.wait("vector", ST, tok)
                            P.wait("vector", SV, tok_gn_prev[0])
                            t1 = P.do("vector", lambda V, pY_=pY_: V.reduce_sum(
                                out=gnm[:], in_=pY_, axis=AX.X), (SV, 1))
                            P.wait("scalar", SV, t1)
                            P.wait("scalar", SS, tok_yncons.get(j_ - 2, 0))
                            if j_ < 2:
                                P.wait("scalar", ST, yntr2_carry[j_ % 2])
                            t3 = P.do("scalar", lambda S, pY_=pY_, j_=j_: S.activation(
                                out=ynscr[j_ % 2][:], in_=pY_, func=AF.Square,
                                accum_out=gnv[:]), (SS, 1))
                            P.wait("vector", SV, t1)
                            t2 = P.do("vector", lambda V: V.tensor_scalar_mul(
                                gnm[:], gnm[:], 1.0 / 128.0), (SV, 1))
                            P.wait("vector", SV, t2)
                            tm2 = P.do("vector", lambda V: V.tensor_mul(
                                gnm2[:], gnm[:], gnm[:]), (SV, 1))
                            P.wait("vector", SS, t3)
                            P.wait("vector", SV, tm2)
                            t4 = P.do("vector", lambda V: V.tensor_scalar(
                                out=gnv[:], in0=gnv[:], scalar1=1.0 / 128.0,
                                scalar2=EPS, op0=ALU.mult, op1=ALU.add), (SV, 1))
                            P.wait("vector", SV, t4)
                            t4b = P.do("vector", lambda V: V.tensor_sub(
                                gnv[:], gnv[:], gnm2[:]), (SV, 1))
                            P.wait("vector", SV, t4b)
                            t5 = P.do("vector", lambda V: V.reciprocal(
                                gnv[:], gnv[:]), (SV, 1))
                            P.wait("scalar", SV, t5)
                            t6 = P.do("scalar", lambda S: S.activation(
                                out=gnr[:], in_=gnv[:], func=AF.Sqrt), (SS, 1))
                            P.wait("vector", SS, t6)
                            t7 = P.do("vector", lambda V: V.tensor_scalar(
                                out=gnnmr[:], in0=gnm[:], scalar1=gnr[:], scalar2=-1.0,
                                op0=ALU.mult, op1=ALU.mult), (SV, 1))
                            tok_gn_prev[0] = t7
                            P.wait("scalar", SV, t7)
                            t8_ = P.do("scalar", lambda S, pY_=pY_, j_=j_: S.activation(
                                out=ynscr[j_ % 2][:], in_=pY_, func=AF.Identity,
                                bias=gnnmr[:], scale=gnr[:]), (SS, 1))
                            tok_yncons[j_] = t8_
                            pftok[ybk] = [(SS, t8_)]
                            P.wait("tensor", SS, t8_)
                            bwait("tensor", pftok[3])
                            ttr2 = P.do("tensor", lambda T, j_=j_: T.transpose(
                                pf[3][:, 0:128], ynscr[j_ % 2][:], identf_sb[:]),
                                (ST, 1))
                            if j_ >= 14:
                                yntr2_carry[j_ % 2] = ttr2
                            P.wait("vector", ST, ttr2)
                            tok_yntrv[j_] = P.do("vector", lambda V, j_=j_, h=h:
                                                 V.tensor_scalar(
                                out=ynT[:, j_ * 128:(j_ + 1) * 128],
                                in0=pf[3][:, 0:128],
                                scalar1=gnw_sb[:, h:h + 1], scalar2=gnb_sb[:, h:h + 1],
                                op0=ALU.mult, op1=ALU.add), (SV, 1))
                            pftok[3] = [(SV, tok_yntrv[j_])]

                    emit_scores(0)
                    emit_scores(1)
                    emit_Y(0)
                    emit_scores(2)
                    emit_Y(1)
                    emit_scores(3)
                    emit_Y(2)
                    emit_Y(3)
                    P.wait("vector", SV, tok_yntrv[7])
                    P.wait("vector", SS, tok_silu[3])
                    P.do("vector", lambda V, h=h: V.tensor_mul(
                        A2T[:, h, 0:1024], ynT[:, 0:1024], sTh[:, 0:1024]), (SV, 1))
                    P.wait("vector", SV, tok_yntrv[15])
                    tok_a2t[h] = P.do("vector", lambda V, h=h: V.tensor_mul(
                        A2T[:, h, 1024:2048], ynT[:, 1024:2048], sTh[:, 1024:2048]),
                        (SV, 1))
                    P.wait("sync", SV, tok_a2t[h])
                    P.do("sync", lambda S, h=h: S.dma_start(
                        out=d["a2spill"][:, h, :], in_=A2T[:, h, :]), (DSY, 16))
                    ydone_carry[0] = tok_ydone[2]
                    ydone_carry[1] = tok_ydone[3]
                    tok_head_pe[h] = ST.n
                    tok_head_v[h] = SV.n
                    tok_head_s[h] = SS.n
            E.att_end = (ST.n, SV.n, SS.n)
            if KDBG:
                for nm, buf in [("dbg_yT", yT), ("dbg_a2t", A2T),
                                ("dbg_ynT", ynT), ("dbg_sTh", sTh)]:
                    P.do("sync", lambda S, nm=nm, buf=buf: S.dma_start(
                        out=d[nm], in_=buf[:]), (DSY, 16))
            P.barrier()

            with ExitStack() as R2:
                x_res2 = sball("x_res2", [128, 16, 1024], F32, R2)
                P.do("sync", lambda S: S.dma_start(
                    out=x_res2[:], in_=d["xspill"]), (S_WO, 16))
                with ExitStack() as RW:
                    wob = sball("wob", [128, 8, 1024], BF16, RW)
                    A2T2 = sball("A2T2", [128, 8, 2048], BF16, RW)
                    P.do("sync", lambda S: S.dma_start(
                        out=A2T2[:], in_=d["a2spill"]), (S_WO, 16))
                    pO = [RW.enter_context(nc.psum_tensor("pO0", [128, 512], F32)),
                          RW.enter_context(nc.psum_tensor("pO1", [128, 512], F32))]
                    P.do("gpsimd", lambda G: G.dma_start(
                        out=wob[:], in_=d["wo"]), (S_WOB, 16))
                    P.wait("tensor", S_WOB, S_WOB.n)
                    P.wait("tensor", S_WO, S_WO.n)
                    ot = 0
                    tok_oev = {}
                    for tt in range(NTT):
                        for dg in range(2):
                            bank = ot % 2
                            P.wait("tensor", SV, tok_oev.get(ot - 2, 0))
                            tok = None
                            for hbk in range(8):
                                tok = P.do("tensor", lambda T, tt=tt, dg=dg, hbk=hbk,
                                           bank=bank: T.matmul(
                                               pO[bank][:],
                                               A2T2[:, hbk, tt * 128:(tt + 1) * 128],
                                               wob[:, hbk, dg * 512:(dg + 1) * 512],
                                               start=(hbk == 0), stop=(hbk == 7)),
                                           (ST, 1) if hbk == 7 else None)
                            P.wait("vector", ST, tok)
                            if ot == 0:
                                P.wait("vector", S_WO, S_WO.n)
                            tok_oev[ot] = P.do("vector", lambda V, tt=tt, dg=dg,
                                               bank=bank: V.tensor_tensor(
                                out=x_res2[:, tt, dg * 512:(dg + 1) * 512],
                                in0=pO[bank][:],
                                in1=x_res2[:, tt, dg * 512:(dg + 1) * 512],
                                op=ALU.add), (SV, 1))
                            ot += 1
                    tok_wo_last = ST.n
                P.barrier(skip=("gpsimd",))

                oute = d["out"].rearrange("(tt p) d -> p tt d", p=128)
                if stage >= 3:
                    with ExitStack() as RFc:
                        RF2 = alloc_RF(RFc, "f2")
                        emit_ffn_call(1, x_res2, RF2, dma_gate={
                            "w1": [(ST, E.att_end[0]), (SV, E.att_end[1])],
                            "w2": [(ST, tok_wo_last)]})
                    tok_out = 0
                    for q in range(4):
                        P.wait("sync", SS, E.last_tok_post[q * 4 + 3])
                        P.wait("sync", SV, E.last_evac_by_tg[q])
                        tok_out = P.do("sync", lambda S, q=q: S.dma_start(
                            out=oute[:, q * 4:(q + 1) * 4, :],
                            in_=x_res2[:, q * 4:(q + 1) * 4, :]), (DSY, 16))
                    P.barrier(skip=("sync",))
                    P.wait("sync", DSY, tok_out)
                else:
                    P.barrier()
                    tok_out = P.do("sync", lambda S: S.dma_start(
                        out=oute, in_=x_res2[:]), (DSY, 16))
                    P.wait("sync", DSY, tok_out)

        with nc.Block() as block:
            for ename in PB.ENGS:
                ops = P.streams[ename]
                if not ops:
                    continue

                def make_body(ops):
                    def body(ENG):
                        for f in ops:
                            f(ENG)
                    return body
                getattr(block, ename)(make_body(ops))
    nc._alloc_log = alloc_log
    return nc


def kernel(**inputs):
    X, com = _prep(inputs)
    nc = build_program(STAGE)
    in_maps = []
    for b in range(8):
        m = dict(com)
        m["x_in"] = np.ascontiguousarray(X[b])
        in_maps.append(m)
    res = run_bass_kernel_spmd(nc, in_maps, core_ids=list(range(8)))
    return np.stack([np.asarray(r["out"]) for r in res.results], axis=0)


# version 19
# speedup vs baseline: 1.1971x; 1.1971x over previous
import math
import os
import numpy as np
import ml_dtypes
import concourse.bass as bass
import concourse.mybir as mybir
from concourse.bass_utils import run_bass_kernel_spmd
from contextlib import ExitStack
from types import SimpleNamespace

F32 = mybir.dt.float32
BF16 = mybir.dt.bfloat16
AF = mybir.ActivationFunctionType
ALU = mybir.AluOpType
AX = mybir.AxisListType

STAGE = 3
KDBG = os.environ.get("KDBG", "")
KJOBS = os.environ.get("KJOBS", "")
NTT = 16
NCH = 8
NJOBS = 34
EPS = 1e-5


def _host_consts():
    lin = np.linspace(np.float32(math.log(1.0 / 32.0)),
                      np.float32(math.log(1.0 / 512.0)), 8, dtype=np.float32)
    gammas = (np.float32(1.0) - np.exp(lin.astype(np.float32))).astype(np.float64)
    l = np.arange(2048, dtype=np.float64)
    i = np.arange(64, dtype=np.float64)
    base = (2 * i + 0.4 * 128) / (1.4 * 128)
    scale = base[None, :] ** (l[:, None] / 512.0)
    invfreq = 1.0 / (10000.0 ** (i / 64.0))
    sinus = l[:, None] * invfreq[None, :]
    sin, cos = np.sin(sinus), np.cos(sinus)
    tabs = np.zeros((8, 128, 4, 16, 64), np.float32)
    for h in range(8):
        gp = gammas[h] ** l
        gn = gammas[h] ** (-l)
        for k, tab in enumerate([cos * scale * gp[:, None], sin * scale * gp[:, None],
                                 (cos / scale) * gn[:, None], (sin / scale) * gn[:, None]]):
            tabs[h, :, k] = tab.astype(np.float32).reshape(16, 128, 64).transpose(1, 0, 2)
    tri = np.triu(np.ones((128, 128), np.float32))
    return tabs, tri


def _prep(inputs):
    f32 = np.float32
    g = lambda k: np.asarray(inputs[k], f32)
    X = np.ascontiguousarray(g("L0_attn_X"))
    sW1, sW2 = g("L0_ffn_sW1")[0], g("L0_ffn_sW2")[0]
    sb1, sb2 = g("L0_ffn_sb1")[0], g("L0_ffn_sb2")[0]
    rW1, rW2 = g("L0_ffn_rW1"), g("L0_ffn_rW2")
    rb1, rb2 = g("L0_ffn_rb1"), g("L0_ffn_rb2")
    gW, gb = g("L0_ffn_gW"), g("L0_ffn_gb")
    WQ, WK, WV = g("L0_attn_W_Q"), g("L0_attn_W_K"), g("L0_attn_W_V")
    WG, WO = g("L0_attn_W_G"), g("L0_attn_W_O")
    gnw, gnb = g("L0_attn_gn_weight"), g("L0_attn_gn_bias")

    w1l, w2l, b1l = [], [], []
    for s in range(4):
        w1l.append(sW1[:, s * 1024:(s + 1) * 1024])
        w2l.append(sW2[s * 1024:(s + 1) * 1024, :])
        b1l.append(sb1[s * 1024:(s + 1) * 1024])
    for e in range(15):
        for hh in range(2):
            w1l.append(rW1[e][:, hh * 1024:(hh + 1) * 1024])
            w2l.append(rW2[e][hh * 1024:(hh + 1) * 1024, :])
            b1l.append(rb1[e][hh * 1024:(hh + 1) * 1024])
    tr = lambda w: w.reshape(8, 128, w.shape[1]).transpose(1, 0, 2)
    com = {}
    com["w1jobs"] = np.ascontiguousarray(np.stack([tr(w) for w in w1l]))
    com["w2jobs"] = np.ascontiguousarray(np.stack([tr(w) for w in w2l]))
    com["b1all"] = np.ascontiguousarray(np.stack([b.reshape(8, 128).T for b in b1l], axis=1))
    com["b2rs"] = np.ascontiguousarray(np.concatenate([rb2, sb2.reshape(1, 1024)], 0))
    com["gw"] = np.ascontiguousarray(tr(gW))
    com["gbv"] = np.ascontiguousarray(gb.reshape(1, 15))
    com["wq"] = np.ascontiguousarray(np.stack([tr(WQ[h]) for h in range(8)]))
    com["wk"] = np.ascontiguousarray(np.stack([tr(WK[h]) for h in range(8)]))
    com["wv"] = np.ascontiguousarray(np.stack([tr(WV[h]) for h in range(8)]))
    com["wgm"] = np.ascontiguousarray(
        np.stack([tr(WG[:, h * 128:(h + 1) * 128]) for h in range(8)]))
    com["wo"] = np.ascontiguousarray(tr(WO))
    com["gnwv"] = np.ascontiguousarray(gnw.reshape(8, 128).T)
    com["gnbv"] = np.ascontiguousarray(gnb.reshape(8, 128).T)
    tabs, _ = _host_consts()
    com["tabs"] = np.ascontiguousarray(tabs)
    return X, com


class Sem:
    __slots__ = ("name", "n", "h")

    def __init__(self, name):
        self.name, self.n, self.h = name, 0, None


class PB:
    ENGS = ("sync", "gpsimd", "vector", "scalar", "tensor")

    def __init__(self):
        self.streams = {e: [] for e in self.ENGS}
        self.sems = []

    def sem(self, name):
        s = Sem(name)
        self.sems.append(s)
        return s

    def wait(self, eng, sem, val):
        if val is None or val <= 0:
            return
        self.streams[eng].append(lambda E, s=sem, v=val: E.wait_ge(s.h, v))

    def do(self, eng, build, inc=None):
        tok = None
        if inc is not None:
            s, step = inc
            s.n += step
            tok = s.n

        def fn(E, b=build, i=inc):
            ins = b(E)
            if i is not None:
                ins.then_inc(i[0].h, i[1])
        self.streams[eng].append(fn)
        return tok

    def barrier(self, skip=()):
        toks = [(s, s.n) for s in self.sems if s.n > 0]
        for e in self.ENGS:
            if e in skip:
                continue
            for s, v in toks:
                self.wait(e, s, v)


def build_program(stage=STAGE):
    nc = bass.Bass("TRN2", target_bir_lowering=False)
    P = PB()

    d = {}
    d["x_in"] = nc.dram_tensor("x_in", [2048, 1024], F32, kind="ExternalInput").ap()
    d["w1jobs"] = nc.dram_tensor("w1jobs", [34, 128, 8, 1024], F32, kind="ExternalInput").ap()
    d["w2jobs"] = nc.dram_tensor("w2jobs", [34, 128, 8, 1024], F32, kind="ExternalInput").ap()
    d["b1all"] = nc.dram_tensor("b1all", [128, 34, 8], F32, kind="ExternalInput").ap()
    d["b2rs"] = nc.dram_tensor("b2rs", [16, 1024], F32, kind="ExternalInput").ap()
    d["gw"] = nc.dram_tensor("gw", [128, 8, 15], F32, kind="ExternalInput").ap()
    d["gbv"] = nc.dram_tensor("gbv", [1, 15], F32, kind="ExternalInput").ap()
    for k in ("wq", "wk", "wv", "wgm"):
        d[k] = nc.dram_tensor(k, [8, 128, 8, 128], F32, kind="ExternalInput").ap()
    d["wo"] = nc.dram_tensor("wo", [128, 8, 1024], F32, kind="ExternalInput").ap()
    d["gnwv"] = nc.dram_tensor("gnwv", [128, 8], F32, kind="ExternalInput").ap()
    d["gnbv"] = nc.dram_tensor("gnbv", [128, 8], F32, kind="ExternalInput").ap()
    d["tabs"] = nc.dram_tensor("tabs", [8, 128, 4, 16, 64], F32, kind="ExternalInput").ap()
    d["out"] = nc.dram_tensor("out", [2048, 1024], F32, kind="ExternalOutput").ap()
    d["xspill"] = nc.dram_tensor("xspill", [128, 16384], F32, kind="Internal").ap()
    d["a2spill"] = nc.dram_tensor("a2spill", [128, 8, 2048], BF16, kind="Internal").ap()
    if KDBG:
        d["dbg_yT"] = nc.dram_tensor("dbg_yT", [128, 8, 2048], BF16, kind="ExternalOutput").ap()
        d["dbg_g05"] = nc.dram_tensor("dbg_g05", [128, 16, 15], F32, kind="ExternalOutput").ap()
        d["dbg_gT"] = nc.dram_tensor("dbg_gT", [16, 2048], F32, kind="ExternalOutput").ap()
        d["dbg_h0"] = nc.dram_tensor("dbg_h0", [128, 8, 512], BF16, kind="ExternalOutput").ap()
        d["dbg_h1"] = nc.dram_tensor("dbg_h1", [128, 8, 512], BF16, kind="ExternalOutput").ap()
        d["dbg_a2t"] = nc.dram_tensor("dbg_a2t", [128, 8, 2048], BF16, kind="ExternalOutput").ap()
        d["dbg_qt"] = nc.dram_tensor("dbg_qt", [128, 2048], BF16, kind="ExternalOutput").ap()
        d["dbg_kt"] = nc.dram_tensor("dbg_kt", [128, 2048], BF16, kind="ExternalOutput").ap()
        d["dbg_vt"] = nc.dram_tensor("dbg_vt", [128, 16, 128], BF16, kind="ExternalOutput").ap()
        d["dbg_ynT"] = nc.dram_tensor("dbg_ynT", [128, 2048], F32, kind="ExternalOutput").ap()
        d["dbg_sTh"] = nc.dram_tensor("dbg_sTh", [128, 2048], F32, kind="ExternalOutput").ap()
        d["dbg_scb"] = nc.dram_tensor("dbg_scb", [128, 16, 512], BF16, kind="ExternalOutput").ap()
        d["dbg_wq"] = nc.dram_tensor("dbg_wq", [128, 8, 128], BF16, kind="ExternalOutput").ap()
        d["dbg_qt2"] = nc.dram_tensor("dbg_qt2", [128, 2048], BF16, kind="ExternalOutput").ap()

    _, tri_np = _host_consts()
    d["identb"] = nc.inline_tensor(np.eye(128, dtype=ml_dtypes.bfloat16), name="identb").ap()
    d["identf"] = nc.inline_tensor(np.eye(128, dtype=np.float32), name="identf").ap()
    d["tri"] = nc.inline_tensor(tri_np, name="tri").ap()

    SV, SS, ST = P.sem("SV"), P.sem("SS"), P.sem("ST")
    DSY, DGP = P.sem("DSY"), P.sem("DGP")
    S_W = [P.sem("S_W0"), P.sem("S_W1")]
    S_X = [P.sem(f"S_X{i}") for i in range(4)]
    S_PRE = P.sem("S_PRE")
    S_AW = [P.sem("S_AW0"), P.sem("S_AW1")]
    S_SPL = P.sem("S_SPL")
    S_WO = P.sem("S_WO")
    S_WOB = P.sem("S_WOB")
    S_ID = P.sem("S_ID")

    E = SimpleNamespace(atile=0, btile=0, gtg=0, gjob=0,
                        tok_gelu={}, tok_btile={}, tok_evac={},
                        tok_gtg_gelu={}, tok_gtg_bdone={}, tok_job_bdone={},
                        last_evac_by_tg={})

    alloc_log = []
    with ExitStack() as OUTER:
        def sball(name, shape, dt, ctx):
            b = nc.sbuf_bytes_remaining
            t = ctx.enter_context(nc.sbuf_tensor(name, shape, dt))
            alloc_log.append((name, b, nc.sbuf_bytes_remaining))
            return t
        for s in P.sems:
            s.h = OUTER.enter_context(nc.semaphore(s.name))

        yT = sball("yT", [128, 8, 2048], BF16, OUTER)
        stats_m = sball("stats_m", [128, 16], F32, OUTER)
        stats_r = sball("stats_r", [128, 16], F32, OUTER)
        stats_nmr = sball("stats_nmr", [128, 16], F32, OUTER)
        scr_ms = sball("scr_ms", [128, 16], F32, OUTER)
        scr_vr = sball("scr_vr", [128, 16], F32, OUTER)
        post_s2 = sball("post_s2", [128, 16], F32, OUTER)
        post_b2 = sball("post_b2", [128, 16], F32, OUTER)
        b1sb = sball("b1sb", [128, 34, 8], F32, OUTER)
        b2rs_sb = sball("b2rs_sb", [16, 1024], F32, OUTER)
        gw_sb = sball("gw_sb", [128, 8, 15], F32, OUTER)
        gb_sb = sball("gb_sb", [1, 15], F32, OUTER)
        gnw_sb = sball("gnw_sb", [128, 8], F32, OUTER)
        gnb_sb = sball("gnb_sb", [128, 8], F32, OUTER)
        ones_sb = sball("ones_sb", [1, 128], F32, OUTER)
        identb_sb = sball("identb_sb", [128, 128], BF16, OUTER)
        identf_sb = sball("identf_sb", [128, 128], F32, OUTER)
        tri_sb = sball("tri_sb", [128, 128], F32, OUTER)

        P.do("vector", lambda V: V.memset(ones_sb[:], 1.0), (SV, 1))

        def emit_ln(x_res, psT0, psT1, y_scr, sq_scr, bf_direct, gate, tok_x=None, pre_y=None):
            tok_red, tok_m, tok_sq, tok_rec, tok_r, tok_nmr = {}, {}, {}, {}, {}, {}
            tok_ytile = {}
            tok_bank = {0: (SV, 0), 1: (SV, 0)}
            state = {"scr_free": 0, "yTlast": 0, "first": True, "yscr_w": 0}
            tok_gmm, tok_exp, tok_vgate = {}, {}, {}
            tok_yT_tg = {}
            ident = identb_sb if bf_direct else identf_sb
            if gate is not None:
                P.do("vector", lambda V: V.memset(gate["gated"][:, 15:16], 1.0), (SV, 1))

            def s_red(tt):
                if tok_x is not None:
                    P.wait("vector", tok_x[tt][0], tok_x[tt][1])
                tok_red[tt] = P.do("vector", lambda V, tt=tt: V.reduce_sum(
                    out=scr_ms[:, tt:tt + 1], in_=x_res[:, tt, :], axis=AX.X), (SV, 1))
                P.wait("vector", SV, tok_red[tt])
                tok_m[tt] = P.do("vector", lambda V, tt=tt: V.tensor_scalar_mul(
                    stats_m[:, tt:tt + 1], scr_ms[:, tt:tt + 1], 1.0 / 1024.0), (SV, 1))

            def s_sq(tt):
                P.wait("scalar", SV, tok_m[tt])
                P.wait("scalar", SS, state["yscr_w"])
                tok_sq[tt] = P.do("scalar", lambda S, tt=tt: S.activation(
                    out=sq_scr[:], in_=x_res[:, tt, :], func=AF.Square,
                    bias=stats_m[:, tt:tt + 1], scale=-1.0,
                    accum_out=scr_vr[:, tt:tt + 1]), (SS, 1))
                state["yscr_w"] = tok_sq[tt]

            def s_var(tt):
                P.wait("vector", SS, tok_sq[tt])
                t1 = P.do("vector", lambda V, tt=tt: V.tensor_scalar(
                    out=scr_vr[:, tt:tt + 1], in0=scr_vr[:, tt:tt + 1],
                    scalar1=1.0 / 1024.0, scalar2=EPS, op0=ALU.mult, op1=ALU.add), (SV, 1))
                P.wait("vector", SV, t1)
                tok_rec[tt] = P.do("vector", lambda V, tt=tt: V.reciprocal(
                    scr_vr[:, tt:tt + 1], scr_vr[:, tt:tt + 1]), (SV, 1))

            def s_sqrt(tt):
                P.wait("scalar", SV, tok_rec[tt])
                tok_r[tt] = P.do("scalar", lambda S, tt=tt: S.activation(
                    out=stats_r[:, tt:tt + 1], in_=scr_vr[:, tt:tt + 1],
                    func=AF.Sqrt), (SS, 1))

            def s_nmr(tt):
                P.wait("vector", SS, tok_r[tt])
                tok_nmr[tt] = P.do("vector", lambda V, tt=tt: V.tensor_scalar(
                    out=stats_nmr[:, tt:tt + 1], in0=stats_m[:, tt:tt + 1],
                    scalar1=stats_r[:, tt:tt + 1], scalar2=-1.0,
                    op0=ALU.mult, op1=ALU.mult), (SV, 1))

            def s_y(tt):
                if state["first"]:
                    state["first"] = False
                    if pre_y:
                        for eng_, sem_, val_ in pre_y:
                            P.wait(eng_, sem_, val_)
                P.wait("scalar", SV, tok_nmr[tt])
                P.wait("scalar", SS, state["yscr_w"])
                P.wait("scalar", ST, state["scr_free"])
                tok_ytile[tt] = P.do("scalar", lambda S, tt=tt: S.activation(
                    out=y_scr[:], in_=x_res[:, tt, :], func=AF.Identity,
                    bias=stats_nmr[:, tt:tt + 1], scale=stats_r[:, tt:tt + 1]), (SS, 1))
                state["yscr_w"] = tok_ytile[tt]
                for c in range(NCH):
                    bank = c % 2
                    ps = psT0 if bank == 0 else psT1
                    if tt == 0 and c == 0:
                        P.wait("tensor", S_ID, S_ID.n)
                    P.wait("tensor", SS, tok_ytile[tt])
                    P.wait("tensor", tok_bank[bank][0], tok_bank[bank][1])
                    ttr = P.do("tensor", lambda T, ps=ps, c=c: T.transpose(
                        ps[:], y_scr[:, c * 128:(c + 1) * 128], ident[:]), (ST, 1))
                    state["scr_free"] = ttr
                    P.wait("vector", ST, ttr)
                    tcp = P.do("vector", lambda V, ps=ps, tt=tt, c=c: V.tensor_copy(
                        out=yT[:, c, tt * 128:(tt + 1) * 128], in_=ps[:]), (SV, 1))
                    state["yTlast"] = tcp
                    tok_bank[bank] = (SV, tcp)
                    if gate is not None:
                        P.wait("scalar", ST, tok_gmm.get(tt - 1, 0))
                        P.wait("scalar", SV, tcp)
                        tcs = P.do("scalar", lambda S, ps=ps, c=c: S.activation(
                            out=gate["yTf"][:, c, :], in_=ps[:], func=AF.Identity),
                            (SS, 1))
                        tok_bank[bank] = (SS, tcs)
                if tt % 4 == 3:
                    tok_yT_tg[tt // 4] = state["yTlast"]
                if gate is None:
                    return
                pG, pGT = gate["pG"], gate["pGT"]
                if tt == 0:
                    P.wait("tensor", S_PRE, S_PRE.n)
                P.wait("tensor", tok_bank[0][0], tok_bank[0][1])
                P.wait("tensor", tok_bank[1][0], tok_bank[1][1])
                P.wait("tensor", SS, tok_exp.get(tt - 1, 0))
                for c in range(NCH):
                    P.do("tensor", lambda T, c=c: T.matmul(
                        pG[:], gate["yTf"][:, c, :], gw_sb[:, c, :],
                        start=(c == 0), stop=False))
                tok_gmm[tt] = P.do("tensor", lambda T: T.matmul(
                    pG[:], ones_sb[:], gb_sb[:], start=False, stop=True), (ST, 1))
                P.wait("vector", ST, tok_gmm[tt])
                tnm = P.do("vector", lambda V: V.reduce_max(
                    out=gate["nmax"][:], in_=pG[:], axis=AX.X, negate=True), (SV, 1))
                P.wait("scalar", SV, tnm)
                P.wait("scalar", SV, tok_vgate.get(tt - 1, 0))
                tok_exp[tt] = P.do("scalar", lambda S: S.activation(
                    out=gate["e15"][:], in_=pG[:], func=AF.Exp,
                    bias=gate["nmax"][:], scale=1.0), (SS, 1))
                P.wait("vector", SS, tok_exp[tt])
                P.wait("vector", SV, tok_vgate.get(tt - 1, 0))
                t = P.do("vector", lambda V: V.reduce_sum(
                    out=gate["ssum"][:], in_=gate["e15"][:], axis=AX.X), (SV, 1))
                P.wait("vector", SV, t)
                t = P.do("vector", lambda V: V.reciprocal(
                    gate["ssum"][:], gate["ssum"][:]), (SV, 1))
                P.wait("vector", SV, t)
                t = P.do("vector", lambda V: V.tensor_scalar(
                    out=gate["w15"][:], in0=gate["e15"][:], scalar1=gate["ssum"][:],
                    scalar2=None, op0=ALU.mult), (SV, 1))
                P.wait("vector", SV, t)
                t = P.do("vector", lambda V: V.max(
                    out=gate["t8"][:], in_=gate["w15"][:]), (SV, 1))
                P.wait("vector", SV, t)
                t = P.do("vector", lambda V: V.tensor_scalar(
                    out=gate["mask"][:], in0=gate["w15"][:],
                    scalar1=gate["t8"][:, 2:3], scalar2=None, op0=ALU.is_ge), (SV, 1))
                P.wait("vector", SV, t)
                t = P.do("vector", lambda V, tt=tt: V.scalar_tensor_tensor(
                    out=gate["g05"][:, tt, :], in0=gate["mask"][:], scalar=0.5,
                    in1=gate["w15"][:], op0=ALU.mult, op1=ALU.mult), (SV, 1))
                P.wait("vector", SV, t)
                t = P.do("vector", lambda V: V.tensor_mul(
                    gate["gated"][:, 0:15], gate["mask"][:], gate["w15"][:]), (SV, 1))
                P.wait("tensor", SV, t)
                P.wait("tensor", SV, tok_vgate.get(tt - 1, 0))
                tg2 = P.do("tensor", lambda T: T.transpose(
                    pGT[:], gate["gated"][:], identf_sb[:]), (ST, 1))
                P.wait("vector", ST, tg2)
                tok_vgate[tt] = P.do("vector", lambda V, tt=tt: V.tensor_copy(
                    out=gate["gatedT"][0:16, tt * 128:(tt + 1) * 128],
                    in_=pGT[:]), (SV, 1))

            for step in range(NTT + 3):
                if step < NTT:
                    s_red(step)
                if 0 <= step - 1 < NTT:
                    s_sq(step - 1)
                    s_var(step - 1)
                if 0 <= step - 2 < NTT:
                    s_sqrt(step - 2)
                    s_nmr(step - 2)
                if 0 <= step - 3 < NTT:
                    s_y(step - 3)
            return SimpleNamespace(
                tok_y_last=tok_ytile[NTT - 1],
                tok_yT=state["yTlast"],
                tok_yT_tg=tok_yT_tg,
                tok_gate=tok_vgate.get(NTT - 1, 0),
                tok_tr_last=state["scr_free"])

        def emit_ffn_call(call, x_res, RF, dma_gate=None, tok_x=None):
            w1b = [RF["w1b0"], RF["w1b1"]]
            w2b = [RF["w2b0"], RF["w2b1"]]
            hb = [RF["h0"], RF["h1"]]
            pA = [RF["pA0"], RF["pA1"]]
            pB = [RF["pB0"], RF["pB1"]]
            gate = {k: RF[k] for k in ("yTf", "pG", "pGT", "nmax", "e15", "ssum",
                                       "w15", "t8", "mask", "gated", "g05", "gatedT")}
            ln = emit_ln(x_res, RF["pT0"], RF["pT1"], RF["y_scr"], RF["y_scr"], False, gate,
                         tok_x=tok_x)

            tok_pb = {}
            for tt in range(NTT):
                P.do("vector", lambda V, tt=tt: V.tensor_scalar(
                    out=post_s2[:, tt:tt + 1], in0=stats_r[:, tt:tt + 1],
                    scalar1=0.5, scalar2=1.0, op0=ALU.mult, op1=ALU.add), (SV, 1))
                tok_pb[tt] = P.do("vector", lambda V, tt=tt: V.tensor_scalar_mul(
                    post_b2[:, tt:tt + 1], stats_nmr[:, tt:tt + 1], 0.5), (SV, 1))
            tok_post = {}
            for tt in range(NTT):
                P.wait("scalar", SV, tok_pb[tt])
                tok_post[tt] = P.do("scalar", lambda S, tt=tt: S.activation(
                    out=x_res[:, tt, :], in_=x_res[:, tt, :], func=AF.Identity,
                    bias=post_b2[:, tt:tt + 1], scale=post_s2[:, tt:tt + 1]), (SS, 1))
            tok_post_last = tok_post[NTT - 1]
            E.last_tok_post = tok_post

            if KJOBS == "shared":
                job_list = [0, 1, 2, 3]
            elif KJOBS.startswith("j"):
                job_list = [int(t) for t in KJOBS[1:].split(",")]
            else:
                job_list = list(range(NJOBS))
            tok_w = {}

            def emit_wdma(jj, j):
                gj = E.gjob
                if dma_gate is not None and jj <= 1:
                    for s_, v_ in dma_gate.get("w1", []):
                        P.wait("gpsimd", s_, v_)
                if gj >= 2:
                    P.wait("gpsimd", ST, E.tok_job_bdone[gj - 2])
                P.do("gpsimd", lambda G, j=j: G.dma_start(
                    out=w1b[j % 2][:], in_=d["w1jobs"][j]), (S_W[j % 2], 16))
                if dma_gate is not None and jj <= 1:
                    for s_, v_ in dma_gate.get("w2", []):
                        P.wait("gpsimd", s_, v_)
                tok_w[j] = P.do("gpsimd", lambda G, j=j: G.dma_start(
                    out=w2b[j % 2][:], in_=d["w2jobs"][j]), (S_W[j % 2], 16))

            first_evac = [True]

            def emit_A(j, tg, gtg):
                war = E.tok_gtg_bdone.get(gtg - 2, 0)
                for ft in range(NCH):
                    at = E.atile
                    P.wait("tensor", SS, E.tok_gelu.get(at - 2, 0))
                    if ft == 0 and tg == 0:
                        P.wait("tensor", S_W[j % 2], tok_w[j])
                    if ft == 0 and j == job_list[0]:
                        P.wait("tensor", SV, ln.tok_yT_tg[tg])
                    tok = None
                    for c in range(NCH):
                        tok = P.do("tensor", lambda T, j=j, tg=tg, ft=ft, c=c, at=at:
                                   T.matmul(pA[at % 2][:],
                                            w1b[j % 2][:, c, ft * 128:(ft + 1) * 128],
                                            yT[:, c, tg * 512:(tg + 1) * 512],
                                            start=(c == 0), stop=(c == NCH - 1)),
                                   (ST, 1) if c == NCH - 1 else None)
                    P.wait("scalar", ST, tok)
                    if at == 0:
                        P.wait("scalar", S_PRE, S_PRE.n)
                    if ft == 0:
                        P.wait("scalar", ST, war)
                    E.tok_gelu[at] = P.do("scalar", lambda S, j=j, ft=ft, at=at, gtg=gtg:
                                          S.activation(
                        out=hb[gtg % 2][:, ft, :], in_=pA[at % 2][:], func=AF.Gelu,
                        bias=b1sb[:, j, ft:ft + 1], scale=1.0), (SS, 1))
                    E.atile += 1
                E.tok_gtg_gelu[gtg] = E.tok_gelu[E.atile - 1]

            def emit_B(j, tg, gtg, kind, eidx):
                P.wait("tensor", SS, E.tok_gtg_gelu[gtg])
                has_bias = (kind == "shared" and j == 3)
                nmm = NCH + (1 if has_bias else 0)
                for tl in range(4):
                    ttg = tg * 4 + tl
                    for dg in range(2):
                        bt = E.btile
                        P.wait("tensor", SV, E.tok_evac.get(bt - 2, 0))
                        tok = None
                        for k in range(nmm):
                            il = (k == nmm - 1)
                            if k < NCH:
                                tok = P.do("tensor",
                                           lambda T, tl=tl, dg=dg, ft=k, bt=bt, il=il,
                                           gtg=gtg, j=j: T.matmul(
                                               pB[bt % 2][:],
                                               hb[gtg % 2][:, ft, tl * 128:(tl + 1) * 128],
                                               w2b[j % 2][:, ft, dg * 512:(dg + 1) * 512],
                                               start=(ft == 0), stop=il),
                                           (ST, 1) if il else None)
                            else:
                                if ttg == 0 and dg == 0:
                                    P.wait("tensor", S_PRE, S_PRE.n)
                                    P.wait("tensor", SV, ln.tok_gate)
                                tok = P.do("tensor",
                                           lambda T, ttg=ttg, dg=dg, bt=bt, il=il:
                                           T.matmul(
                                               pB[bt % 2][:],
                                               RF["gatedT"][0:16, ttg * 128:(ttg + 1) * 128],
                                               b2rs_sb[0:16, dg * 512:(dg + 1) * 512],
                                               start=False, stop=il),
                                           (ST, 1) if il else None)
                        E.tok_btile[bt] = tok
                        P.wait("vector", ST, tok)
                        if first_evac[0]:
                            P.wait("vector", SS, tok_post_last)
                            P.wait("vector", SV, ln.tok_gate)
                            first_evac[0] = False
                        if kind == "shared":
                            ev = P.do("vector", lambda V, ttg=ttg, dg=dg, bt=bt:
                                      V.scalar_tensor_tensor(
                                          out=x_res[:, ttg, dg * 512:(dg + 1) * 512],
                                          in0=pB[bt % 2][:], scalar=0.5,
                                          in1=x_res[:, ttg, dg * 512:(dg + 1) * 512],
                                          op0=ALU.mult, op1=ALU.add), (SV, 1))
                        else:
                            ev = P.do("vector", lambda V, ttg=ttg, dg=dg, bt=bt, e=eidx:
                                      V.scalar_tensor_tensor(
                                          out=x_res[:, ttg, dg * 512:(dg + 1) * 512],
                                          in0=pB[bt % 2][:],
                                          scalar=RF["g05"][:, ttg, e:e + 1],
                                          in1=x_res[:, ttg, dg * 512:(dg + 1) * 512],
                                          op0=ALU.mult, op1=ALU.add), (SV, 1))
                        E.tok_evac[bt] = ev
                        E.last_evac_by_tg[ttg // 4] = ev
                        E.btile += 1

            for jj, j in enumerate(job_list):
                emit_wdma(jj, j)
                kind = "shared" if j < 4 else "router"
                eidx = None if j < 4 else (j - 4) // 2
                gtg_of = {}
                for what, tg in [("A", 0), ("A", 1), ("B", 0), ("A", 2), ("B", 1),
                                 ("A", 3), ("B", 2), ("B", 3)]:
                    if what == "A":
                        gtg_of[tg] = E.gtg
                        emit_A(j, tg, E.gtg)
                        E.gtg += 1
                    else:
                        emit_B(j, tg, gtg_of[tg], kind, eidx)
                        E.tok_gtg_bdone[gtg_of[tg]] = E.tok_btile[E.btile - 1]
                E.tok_job_bdone[E.gjob] = E.tok_btile[E.btile - 1]
                E.gjob += 1

            tok_all_evac = E.tok_evac[E.btile - 1]
            return tok_all_evac

        def alloc_RF(ctx, pfx):
            RF = {}
            RF["w2b0"] = sball(pfx + "w2b0", [128, 8, 1024], BF16, ctx)
            RF["w2b1"] = sball(pfx + "w2b1", [128, 8, 1024], BF16, ctx)
            RF["h0"] = sball(pfx + "h0", [128, 8, 512], BF16, ctx)
            RF["h1"] = sball(pfx + "h1", [128, 8, 512], BF16, ctx)
            RF["y_scr"] = sball(pfx + "y_scr", [128, 1024], F32, ctx)
            RF["yTf"] = sball(pfx + "yTf", [128, 8, 128], F32, ctx)
            RF["nmax"] = sball(pfx + "nmax", [128, 1], F32, ctx)
            RF["e15"] = sball(pfx + "e15", [128, 15], F32, ctx)
            RF["ssum"] = sball(pfx + "ssum", [128, 1], F32, ctx)
            RF["w15"] = sball(pfx + "w15", [128, 15], F32, ctx)
            RF["t8"] = sball(pfx + "t8", [128, 8], F32, ctx)
            RF["mask"] = sball(pfx + "mask", [128, 15], F32, ctx)
            RF["gated"] = sball(pfx + "gated", [128, 16], F32, ctx)
            RF["g05"] = sball(pfx + "g05", [128, 16, 15], F32, ctx)
            RF["gatedT"] = sball(pfx + "gatedT", [16, 2048], F32, ctx)
            RF["w1b0"] = sball(pfx + "w1b0", [128, 8, 1024], BF16, ctx)
            RF["w1b1"] = sball(pfx + "w1b1", [128, 8, 1024], BF16, ctx)
            for nm, shape, dt in [("pA0", [128, 512], F32), ("pA1", [128, 512], F32),
                                  ("pB0", [128, 512], F32), ("pB1", [128, 512], F32),
                                  ("pT0", [128, 128], F32), ("pT1", [128, 128], F32),
                                  ("pG", [128, 15], F32), ("pGT", [16, 128], F32)]:
                RF[nm] = ctx.enter_context(nc.psum_tensor(pfx + nm, shape, dt))
            return RF

        with ExitStack() as R1:
            x_res = sball("x_res", [128, 16, 1024], F32, R1)
            xre = d["x_in"].rearrange("(tt p) d -> p tt d", p=128)
            tok_x = {}
            for a_, b_ in [(d["identb"], identb_sb), (d["identf"], identf_sb),
                           (d["tri"], tri_sb)]:
                P.do("sync", lambda S, a=a_, b=b_: S.dma_start(
                    out=b[:], in_=a), (S_ID, 16))
            pre1 = [(d["gw"], gw_sb), (d["gbv"], gb_sb), (d["b1all"], b1sb),
                    (d["b2rs"], b2rs_sb)]
            pre2 = [(d["gnwv"], gnw_sb), (d["gnbv"], gnb_sb)]
            for q in range(4):
                t = P.do("sync", lambda S, q=q: S.dma_start(
                    out=x_res[:, q * 4:(q + 1) * 4, :], in_=xre[:, q * 4:(q + 1) * 4, :]),
                    (S_X[q], 16))
                for tt in range(q * 4, (q + 1) * 4):
                    tok_x[tt] = (S_X[q], t)
                batch = pre1 if q == 0 else (pre2 if q == 1 else [])
                for a_, b_ in batch:
                    P.do("sync", lambda S, a=a_, b=b_: S.dma_start(
                        out=b[:], in_=a), (S_PRE, 16))

            with ExitStack() as RFc:
                RF = alloc_RF(RFc, "f1")
                emit_ffn_call(0, x_res, RF, tok_x=tok_x,
                              dma_gate={"w1": [(S_X[0], 16)]})
            P.barrier(skip=("vector", "scalar", "tensor", "sync"))

            E.f1_done = dict(jb=E.tok_job_bdone[E.gjob - 1],
                             ev=E.tok_evac[E.btile - 1],
                             ge=E.tok_gelu[E.atile - 1],
                             po=E.last_tok_post[NTT - 1],
                             evtg=dict(E.last_evac_by_tg))
            if stage == 1:
                P.barrier(skip=("sync",))
                if KDBG:
                    P.do("sync", lambda S: S.dma_start(out=d["dbg_h0"], in_=RF["h0"][:]), (DSY, 16))
                    P.do("sync", lambda S: S.dma_start(out=d["dbg_h1"], in_=RF["h1"][:]), (DSY, 16))
                    P.do("sync", lambda S: S.dma_start(out=d["dbg_yT"], in_=yT[:]), (DSY, 16))
                    P.do("sync", lambda S: S.dma_start(out=d["dbg_g05"], in_=RF["g05"][:]), (DSY, 16))
                    P.do("sync", lambda S: S.dma_start(out=d["dbg_gT"], in_=RF["gatedT"][:]), (DSY, 16))
                for q in range(4):
                    P.wait("sync", SV, E.f1_done["evtg"][q])
                    P.wait("sync", SS, E.f1_done["po"])
                    tok_out = P.do("sync", lambda S, q=q: S.dma_start(
                        out=d["out"].rearrange("(tt p) d -> p tt d", p=128)[
                            :, q * 4:(q + 1) * 4, :],
                        in_=x_res[:, q * 4:(q + 1) * 4, :]), (DSY, 16))
                P.wait("sync", DSY, tok_out)
            else:
                fd = E.f1_done
                tok_x2 = {tt: (SV, fd["evtg"][tt // 4]) for tt in range(NTT)}
                with ExitStack() as RL2:
                    y2_scr = sball("y2_scr", [128, 1024], BF16, RL2)
                    sq2_scr = sball("sq2_scr", [128, 1024], F32, RL2)
                    pL0 = RL2.enter_context(nc.psum_tensor("pL0", [128, 128], BF16))
                    pL1 = RL2.enter_context(nc.psum_tensor("pL1", [128, 128], BF16))
                    P.wait("scalar", ST, fd["jb"])
                    P.wait("tensor", SV, fd["ev"])
                    P.wait("tensor", SS, fd["ge"])
                    ln2 = emit_ln(x_res, pL0, pL1, y2_scr, sq2_scr, True, None,
                                  tok_x=tok_x2,
                                  pre_y=[("vector", ST, fd["jb"])])
                    E.ln2_yT = ln2.tok_yT
                for q in range(4):
                    P.wait("sync", SV, fd["evtg"][q])
                    P.wait("sync", SS, fd["po"])
                    P.do("sync", lambda S, q=q: S.dma_start(
                        out=d["xspill"][:, q * 4096:(q + 1) * 4096],
                        in_=x_res[:, q * 4:(q + 1) * 4, :]), (S_SPL, 16))

        if stage >= 2:
            with ExitStack() as RA:
                A2T = sball("A2T", [128, 8, 2048], BF16, RA)
                QT = sball("QT", [128, 2048], BF16, RA)
                KT = sball("KT", [128, 2048], BF16, RA)
                Vt = sball("Vt", [128, 16, 128], BF16, RA)
                scb = [sball("scb0", [128, 16, 512], BF16, RA),
                       sball("scb1", [128, 16, 512], BF16, RA)]
                ynT = sball("ynT", [128, 2048], F32, RA)
                sTh = sball("sTh", [128, 2048], F32, RA)
                q32 = [sball("q32a", [128, 512], F32, RA),
                       sball("q32b", [128, 512], F32, RA)]
                qbf = [sball("qbfa", [128, 512], BF16, RA),
                       sball("qbfb", [128, 512], BF16, RA)]
                xa4 = sball("xa4", [128, 256], F32, RA)
                xb4 = sball("xb4", [128, 256], F32, RA)
                ynscr = [sball("ynscr0", [128, 128], F32, RA),
                         sball("ynscr1", [128, 128], F32, RA)]
                gnm = sball("gnm", [128, 1], F32, RA)
                gnv = sball("gnv", [128, 1], F32, RA)
                gnr = sball("gnr", [128, 1], F32, RA)
                gnnmr = sball("gnnmr", [128, 1], F32, RA)
                gnm2 = sball("gnm2", [128, 1], F32, RA)
                tabb = [sball("tabb0", [128, 4, 16, 64], F32, RA),
                        sball("tabb1", [128, 4, 16, 64], F32, RA)]
                wqb = [sball("wqb0", [128, 8, 128], BF16, RA),
                       sball("wqb1", [128, 8, 128], BF16, RA)]
                wkb = [sball("wkb0", [128, 8, 128], BF16, RA),
                       sball("wkb1", [128, 8, 128], BF16, RA)]
                wvb = [sball("wvb0", [128, 8, 128], BF16, RA),
                       sball("wvb1", [128, 8, 128], BF16, RA)]
                wgb = [sball("wgb0", [128, 8, 128], BF16, RA),
                       sball("wgb1", [128, 8, 128], BF16, RA)]
                pf = [RA.enter_context(nc.psum_tensor(f"pf{i}", [128, 512], F32))
                      for i in range(6)]
                pb2 = [RA.enter_context(nc.psum_tensor(f"pb{i}", [128, 128], BF16))
                       for i in range(2)]
                pftok = [[] for _ in range(6)]
                pbtok = [[] for _ in range(2)]

                def bwait(eng, toks):
                    for s, v in toks:
                        P.wait(eng, s, v)

                qev, qov, bev, bov = [], [], [], []
                for par in range(2):
                    v32 = q32[par][:].rearrange("p (t e) -> p t e", t=4)
                    vbf = qbf[par][:].rearrange("p (t e) -> p t e", t=4)
                    qev.append(v32[:, :, 0:128:2])
                    qov.append(v32[:, :, 1:128:2])
                    bev.append(vbf[:, :, 0:128:2])
                    bov.append(vbf[:, :, 1:128:2])
                xa4v = xa4[:].rearrange("p (t e) -> p t e", t=4)
                xb4v = xb4[:].rearrange("p (t e) -> p t e", t=4)

                tok_head_pe, tok_head_v, tok_head_s, tok_a2t = {}, {}, {}, {}
                qkv_carry = {"ttr_last": [0, 0], "chain_last": [0, 0]}
                gn_prev_global = [0]
                ydone_carry = [0, 0]
                yntr2_carry = [0, 0]

                for h in range(8):
                    if h == 0:
                        P.wait("vector", S_SPL, S_SPL.n)
                        P.wait("scalar", S_SPL, S_SPL.n)
                        P.wait("tensor", SV, max(E.f1_done["ev"], E.ln2_yT))
                        P.wait("tensor", SS, E.f1_done["ge"])
                    if h >= 2:
                        P.wait("gpsimd", ST, tok_head_pe.get(h - 2, 0))
                        P.wait("gpsimd", SV, tok_head_v.get(h - 2, 0))
                        P.wait("gpsimd", SS, tok_head_s.get(h - 2, 0))
                    P.do("gpsimd", lambda G, h=h: G.dma_start(
                        out=tabb[h % 2][:], in_=d["tabs"][h]), (S_AW[h % 2], 16))
                    for src_, dstl in [("wq", wqb), ("wk", wkb), ("wv", wvb),
                                       ("wgm", wgb)]:
                        P.do("gpsimd", lambda G, s=src_, dl=dstl, h=h: G.dma_start(
                            out=dl[h % 2][:], in_=d[s][h]), (S_AW[h % 2], 16))
                    tok_hw = S_AW[h % 2].n

                    P.wait("tensor", S_AW[h % 2], tok_hw)

                    tok_silu = {}
                    for g4 in range(4):
                        bk = 4 + (g4 % 2)
                        bwait("tensor", pftok[bk])
                        tok = None
                        for c in range(NCH):
                            tok = P.do("tensor", lambda T, g4=g4, c=c, h=h, bk=bk:
                                       T.matmul(pf[bk][:], wgb[h % 2][:, c, :],
                                                yT[:, c, g4 * 512:(g4 + 1) * 512],
                                                start=(c == 0), stop=(c == NCH - 1)),
                                       (ST, 1) if c == NCH - 1 else None)
                        P.wait("scalar", ST, tok)
                        if g4 == 0:
                            P.wait("scalar", SV, tok_a2t.get(h - 1, 0))
                        tok_silu[g4] = P.do("scalar", lambda S, g4=g4, bk=bk:
                                            S.activation(
                            out=sTh[:, g4 * 512:(g4 + 1) * 512],
                            in_=pf[bk][:], func=AF.Silu), (SS, 1))
                        pftok[bk] = [(SS, tok_silu[g4])]

                    def qkv2(which, wl, dstT, kc, ks):
                        chain_tok = {}
                        ttr = {}
                        last_sv = [0]

                        def emit_mms(grp):
                            stage_last = 0
                            for t4 in range(4):
                                tt = grp * 4 + t4
                                bk = tt % 4
                                bwait("tensor", pftok[bk])
                                tok = None
                                for c in range(NCH):
                                    tok = P.do("tensor", lambda T, c=c, tt=tt, bk=bk,
                                               h=h, wl=wl: T.matmul(
                                        pf[bk][:, 0:128],
                                        yT[:, c, tt * 128:(tt + 1) * 128],
                                        wl[h % 2][:, c, :],
                                        start=(c == 0), stop=(c == NCH - 1)),
                                        (ST, 1) if c == NCH - 1 else None)
                                if which == "v":
                                    P.wait("vector", ST, tok)
                                    t = P.do("vector", lambda V, tt=tt, bk=bk:
                                             V.tensor_copy(out=Vt[:, tt, :],
                                                           in_=pf[bk][:, 0:128]),
                                             (SV, 1))
                                    pftok[bk] = [(SV, t)]
                                    last_sv[0] = t
                                else:
                                    P.wait("scalar", ST, tok)
                                    if t4 == 0:
                                        prev = chain_tok.get(grp - 2,
                                                             qkv_carry["chain_last"][grp % 2])
                                        P.wait("scalar", SV, prev)
                                    t = P.do("scalar", lambda S, t4=t4, bk=bk, grp=grp:
                                             S.activation(
                                        out=q32[grp % 2][:, t4 * 128:(t4 + 1) * 128],
                                        in_=pf[bk][:, 0:128], func=AF.Identity),
                                        (SS, 1))
                                    pftok[bk] = [(SS, t)]
                                    stage_last = t
                            return stage_last

                        def emit_chain(grp, stage_last):
                            par = grp % 2
                            ct = tabb[h % 2][:, kc, grp * 4:(grp + 1) * 4, :]
                            st_ = tabb[h % 2][:, ks, grp * 4:(grp + 1) * 4, :]
                            P.wait("vector", SS, stage_last)
                            P.wait("vector", SV, qkv_carry.get("prev_chain", 0))
                            P.wait("vector", SV, chain_tok.get(grp - 2,
                                                               qkv_carry["chain_last"][par]))
                            P.wait("vector", ST, ttr.get((grp - 2) * 4 + 3,
                                                         qkv_carry["ttr_last"][par]))
                            P.do("vector", lambda V, ct=ct, par=par: V.tensor_mul(
                                xa4v, qev[par], ct), (SV, 1))
                            t2 = P.do("vector", lambda V, st_=st_, par=par: V.tensor_mul(
                                xb4v, qov[par], st_), (SV, 1))
                            P.wait("vector", SV, t2)
                            t3 = P.do("vector", lambda V, par=par: V.tensor_sub(
                                bev[par], xa4v, xb4v), (SV, 1))
                            P.wait("vector", SV, t3)
                            P.do("vector", lambda V, ct=ct, par=par: V.tensor_mul(
                                xa4v, qov[par], ct), (SV, 1))
                            t5 = P.do("vector", lambda V, st_=st_, par=par: V.tensor_mul(
                                xb4v, qev[par], st_), (SV, 1))
                            P.wait("vector", SV, t5)
                            chain_tok[grp] = P.do("vector", lambda V, par=par: V.tensor_add(
                                bov[par], xa4v, xb4v), (SV, 1))
                            qkv_carry["prev_chain"] = chain_tok[grp]
                            last_sv[0] = chain_tok[grp]

                        def emit_trans(grp):
                            for t4 in range(4):
                                tt = grp * 4 + t4
                                P.wait("tensor", SV, chain_tok[grp])
                                bwait("tensor", pbtok[tt % 2])
                                tk = P.do("tensor", lambda T, t4=t4, tt=tt, grp=grp:
                                          T.transpose(
                                    pb2[tt % 2][:],
                                    qbf[grp % 2][:, t4 * 128:(t4 + 1) * 128],
                                    identb_sb[:]), (ST, 1))
                                ttr[tt] = tk
                                P.wait("vector", ST, tk)
                                tc = P.do("vector", lambda V, tt=tt, dT=dstT:
                                          V.tensor_copy(
                                    out=dT[:, tt * 128:(tt + 1) * 128],
                                    in_=pb2[tt % 2][:]), (SV, 1))
                                pbtok[tt % 2] = [(SV, tc)]
                                last_sv[0] = tc

                        if which == "v":
                            for grp in range(4):
                                emit_mms(grp)
                        else:
                            sl = {}
                            sl[0] = emit_mms(0)
                            emit_chain(0, sl[0])
                            for grp in range(1, 4):
                                sl[grp] = emit_mms(grp)
                                emit_chain(grp, sl[grp])
                                emit_trans(grp - 1)
                            emit_trans(3)
                            qkv_carry["ttr_last"] = [ttr[11], ttr[15]]
                            qkv_carry["chain_last"] = [chain_tok[2], chain_tok[3]]
                        return last_sv[0]

                    tq = qkv2("q", wqb, QT, 0, 1)
                    tk2 = qkv2("k", wkb, KT, 2, 3)
                    tv = qkv2("v", wvb, None, None, None)

                    sc_cnt = 0
                    tok_ydone = {-2: ydone_carry[0], -1: ydone_carry[1]}
                    tok_yncons = {}
                    tok_yntrv = {}
                    tok_gn_prev = gn_prev_global
                    cast_sv = {}
                    cast_ss = {}
                    first_sc = [True]

                    def emit_scores(g):
                        nonlocal sc_cnt
                        cast_sv[g] = 0
                        cast_ss[g] = 0
                        for i in range(4 * g + 4):
                            bk = 4 + (sc_cnt % 2)
                            bwait("tensor", pftok[bk])
                            if first_sc[0]:
                                P.wait("tensor", SV, max(tq, tk2, tv))
                                first_sc[0] = False
                            if i == 0:
                                P.wait("tensor", ST, tok_ydone.get(g - 2, 0))
                            tokm = P.do("tensor", lambda T, i=i, g=g, bk=bk: T.matmul(
                                pf[bk][:], KT[:, i * 128:(i + 1) * 128],
                                QT[:, g * 512:(g + 1) * 512],
                                start=True, stop=True), (ST, 1))
                            gl = g * 4
                            frees = []
                            if i >= gl:
                                off = (i - gl) * 128
                                P.wait("vector", ST, tokm)
                                tdi = P.do("vector", lambda V, i=i, g=g, bk=bk, off=off:
                                           V.tensor_mul(
                                    scb[g % 2][:, i, off:off + 128],
                                    pf[bk][:, off:off + 128], tri_sb[:]), (SV, 1))
                                cast_sv[g] = tdi
                                frees.append((SV, tdi))
                                if off + 128 < 512:
                                    P.wait("scalar", SV, tdi)
                                    tcp = P.do("scalar", lambda S, i=i, g=g, bk=bk,
                                               off=off: S.activation(
                                        out=scb[g % 2][:, i, off + 128:512],
                                        in_=pf[bk][:, off + 128:512],
                                        func=AF.Identity), (SS, 1))
                                    cast_ss[g] = tcp
                                    frees = [(SS, tcp)]
                            else:
                                if sc_cnt % 2 == 0:
                                    P.wait("vector", ST, tokm)
                                    t = P.do("vector", lambda V, i=i, g=g, bk=bk:
                                             V.tensor_copy(
                                        out=scb[g % 2][:, i, 0:512],
                                        in_=pf[bk][:, 0:512]), (SV, 1))
                                    cast_sv[g] = t
                                    frees.append((SV, t))
                                else:
                                    P.wait("scalar", ST, tokm)
                                    t = P.do("scalar", lambda S, i=i, g=g, bk=bk:
                                             S.activation(
                                        out=scb[g % 2][:, i, 0:512],
                                        in_=pf[bk][:, 0:512], func=AF.Identity),
                                        (SS, 1))
                                    cast_ss[g] = t
                                    frees.append((SS, t))
                            pftok[bk] = frees
                            sc_cnt += 1

                    def emit_Y(g):
                        for jl in range(4):
                            j_ = g * 4 + jl
                            ybk = j_ % 3
                            bwait("tensor", pftok[ybk])
                            P.wait("tensor", SV, cast_sv[g])
                            P.wait("tensor", SS, cast_ss[g])
                            tok = None
                            for i in range(j_ + 1):
                                tok = P.do("tensor", lambda T, i=i, jl=jl, g=g,
                                           ybk=ybk, j_=j_: T.matmul(
                                    pf[ybk][:, 0:128],
                                    scb[g % 2][:, i, jl * 128:(jl + 1) * 128],
                                    Vt[:, i, :],
                                    start=(i == 0), stop=(i == j_)),
                                    (ST, 1) if i == j_ else None)
                            if jl == 3:
                                tok_ydone[g] = tok
                            pY_ = pf[ybk][:, 0:128]
                            P.wait("vector", ST, tok)
                            P.wait("vector", SV, tok_gn_prev[0])
                            t1 = P.do("vector", lambda V, pY_=pY_: V.reduce_sum(
                                out=gnm[:], in_=pY_, axis=AX.X), (SV, 1))
                            P.wait("scalar", SV, t1)
                            P.wait("scalar", SS, tok_yncons.get(j_ - 2, 0))
                            if j_ < 2:
                                P.wait("scalar", ST, yntr2_carry[j_ % 2])
                            t3 = P.do("scalar", lambda S, pY_=pY_, j_=j_: S.activation(
                                out=ynscr[j_ % 2][:], in_=pY_, func=AF.Square,
                                accum_out=gnv[:]), (SS, 1))
                            P.wait("vector", SV, t1)
                            t2 = P.do("vector", lambda V: V.tensor_scalar_mul(
                                gnm[:], gnm[:], 1.0 / 128.0), (SV, 1))
                            P.wait("vector", SV, t2)
                            tm2 = P.do("vector", lambda V: V.tensor_mul(
                                gnm2[:], gnm[:], gnm[:]), (SV, 1))
                            P.wait("vector", SS, t3)
                            P.wait("vector", SV, tm2)
                            t4 = P.do("vector", lambda V: V.tensor_scalar(
                                out=gnv[:], in0=gnv[:], scalar1=1.0 / 128.0,
                                scalar2=EPS, op0=ALU.mult, op1=ALU.add), (SV, 1))
                            P.wait("vector", SV, t4)
                            t4b = P.do("vector", lambda V: V.tensor_sub(
                                gnv[:], gnv[:], gnm2[:]), (SV, 1))
                            P.wait("vector", SV, t4b)
                            t5 = P.do("vector", lambda V: V.reciprocal(
                                gnv[:], gnv[:]), (SV, 1))
                            P.wait("scalar", SV, t5)
                            t6 = P.do("scalar", lambda S: S.activation(
                                out=gnr[:], in_=gnv[:], func=AF.Sqrt), (SS, 1))
                            P.wait("vector", SS, t6)
                            t7 = P.do("vector", lambda V: V.tensor_scalar(
                                out=gnnmr[:], in0=gnm[:], scalar1=gnr[:], scalar2=-1.0,
                                op0=ALU.mult, op1=ALU.mult), (SV, 1))
                            tok_gn_prev[0] = t7
                            P.wait("scalar", SV, t7)
                            t8_ = P.do("scalar", lambda S, pY_=pY_, j_=j_: S.activation(
                                out=ynscr[j_ % 2][:], in_=pY_, func=AF.Identity,
                                bias=gnnmr[:], scale=gnr[:]), (SS, 1))
                            tok_yncons[j_] = t8_
                            pftok[ybk] = [(SS, t8_)]
                            P.wait("tensor", SS, t8_)
                            bwait("tensor", pftok[3])
                            ttr2 = P.do("tensor", lambda T, j_=j_: T.transpose(
                                pf[3][:, 0:128], ynscr[j_ % 2][:], identf_sb[:]),
                                (ST, 1))
                            if j_ >= 14:
                                yntr2_carry[j_ % 2] = ttr2
                            P.wait("vector", ST, ttr2)
                            tok_yntrv[j_] = P.do("vector", lambda V, j_=j_, h=h:
                                                 V.tensor_scalar(
                                out=ynT[:, j_ * 128:(j_ + 1) * 128],
                                in0=pf[3][:, 0:128],
                                scalar1=gnw_sb[:, h:h + 1], scalar2=gnb_sb[:, h:h + 1],
                                op0=ALU.mult, op1=ALU.add), (SV, 1))
                            pftok[3] = [(SV, tok_yntrv[j_])]

                    emit_scores(0)
                    emit_scores(1)
                    emit_Y(0)
                    emit_scores(2)
                    emit_Y(1)
                    emit_scores(3)
                    emit_Y(2)
                    emit_Y(3)
                    P.wait("vector", SV, tok_yntrv[7])
                    P.wait("vector", SS, tok_silu[3])
                    P.do("vector", lambda V, h=h: V.tensor_mul(
                        A2T[:, h, 0:1024], ynT[:, 0:1024], sTh[:, 0:1024]), (SV, 1))
                    P.wait("vector", SV, tok_yntrv[15])
                    tok_a2t[h] = P.do("vector", lambda V, h=h: V.tensor_mul(
                        A2T[:, h, 1024:2048], ynT[:, 1024:2048], sTh[:, 1024:2048]),
                        (SV, 1))
                    P.wait("sync", SV, tok_a2t[h])
                    P.do("sync", lambda S, h=h: S.dma_start(
                        out=d["a2spill"][:, h, :], in_=A2T[:, h, :]), (DSY, 16))
                    ydone_carry[0] = tok_ydone[2]
                    ydone_carry[1] = tok_ydone[3]
                    tok_head_pe[h] = ST.n
                    tok_head_v[h] = SV.n
                    tok_head_s[h] = SS.n
            E.att_end = (ST.n, SV.n, SS.n)
            if KDBG:
                for nm, buf in [("dbg_yT", yT), ("dbg_a2t", A2T),
                                ("dbg_ynT", ynT), ("dbg_sTh", sTh)]:
                    P.do("sync", lambda S, nm=nm, buf=buf: S.dma_start(
                        out=d[nm], in_=buf[:]), (DSY, 16))
            P.barrier()

            with ExitStack() as R2:
                x_res2 = sball("x_res2", [128, 16, 1024], F32, R2)
                P.do("sync", lambda S: S.dma_start(
                    out=x_res2[:], in_=d["xspill"]), (S_WO, 16))
                with ExitStack() as RW:
                    wob = sball("wob", [128, 8, 1024], BF16, RW)
                    A2T2 = sball("A2T2", [128, 8, 2048], BF16, RW)
                    P.do("sync", lambda S: S.dma_start(
                        out=A2T2[:], in_=d["a2spill"]), (S_WO, 16))
                    pO = [RW.enter_context(nc.psum_tensor("pO0", [128, 512], F32)),
                          RW.enter_context(nc.psum_tensor("pO1", [128, 512], F32))]
                    P.do("gpsimd", lambda G: G.dma_start(
                        out=wob[:], in_=d["wo"]), (S_WOB, 16))
                    P.wait("tensor", S_WOB, S_WOB.n)
                    P.wait("tensor", S_WO, S_WO.n)
                    ot = 0
                    tok_oev = {}
                    for tt in range(NTT):
                        for dg in range(2):
                            bank = ot % 2
                            P.wait("tensor", SV, tok_oev.get(ot - 2, 0))
                            tok = None
                            for hbk in range(8):
                                tok = P.do("tensor", lambda T, tt=tt, dg=dg, hbk=hbk,
                                           bank=bank: T.matmul(
                                               pO[bank][:],
                                               A2T2[:, hbk, tt * 128:(tt + 1) * 128],
                                               wob[:, hbk, dg * 512:(dg + 1) * 512],
                                               start=(hbk == 0), stop=(hbk == 7)),
                                           (ST, 1) if hbk == 7 else None)
                            P.wait("vector", ST, tok)
                            if ot == 0:
                                P.wait("vector", S_WO, S_WO.n)
                            tok_oev[ot] = P.do("vector", lambda V, tt=tt, dg=dg,
                                               bank=bank: V.tensor_tensor(
                                out=x_res2[:, tt, dg * 512:(dg + 1) * 512],
                                in0=pO[bank][:],
                                in1=x_res2[:, tt, dg * 512:(dg + 1) * 512],
                                op=ALU.add), (SV, 1))
                            ot += 1
                    tok_wo_last = ST.n
                P.barrier(skip=("gpsimd",))

                oute = d["out"].rearrange("(tt p) d -> p tt d", p=128)
                if stage >= 3:
                    with ExitStack() as RFc:
                        RF2 = alloc_RF(RFc, "f2")
                        emit_ffn_call(1, x_res2, RF2, dma_gate={
                            "w1": [(ST, E.att_end[0]), (SV, E.att_end[1])],
                            "w2": [(ST, tok_wo_last)]})
                    tok_out = 0
                    for q in range(4):
                        P.wait("sync", SS, E.last_tok_post[q * 4 + 3])
                        P.wait("sync", SV, E.last_evac_by_tg[q])
                        tok_out = P.do("sync", lambda S, q=q: S.dma_start(
                            out=oute[:, q * 4:(q + 1) * 4, :],
                            in_=x_res2[:, q * 4:(q + 1) * 4, :]), (DSY, 16))
                    P.barrier(skip=("sync",))
                    P.wait("sync", DSY, tok_out)
                else:
                    P.barrier()
                    tok_out = P.do("sync", lambda S: S.dma_start(
                        out=oute, in_=x_res2[:]), (DSY, 16))
                    P.wait("sync", DSY, tok_out)

        with nc.Block() as block:
            for ename in PB.ENGS:
                ops = P.streams[ename]
                if not ops:
                    continue

                def make_body(ops):
                    def body(ENG):
                        for f in ops:
                            f(ENG)
                    return body
                getattr(block, ename)(make_body(ops))
    nc._alloc_log = alloc_log
    return nc


def kernel(**inputs):
    X, com = _prep(inputs)
    nc = build_program(STAGE)
    in_maps = []
    for b in range(8):
        m = dict(com)
        m["x_in"] = np.ascontiguousarray(X[b])
        in_maps.append(m)
    res = run_bass_kernel_spmd(nc, in_maps, core_ids=list(range(8)))
    return np.stack([np.asarray(r["out"]) for r in res.results], axis=0)
